# version 21
# baseline (speedup 1.0000x reference)
import math
import contextlib
import numpy as np
import concourse.bass as bass
import concourse.mybir as mybir
from concourse.bass_types import AP
from concourse.bass_utils import run_bass_kernel_spmd

F32 = mybir.dt.float32
BF16 = mybir.dt.bfloat16
I32 = mybir.dt.int32
AF = mybir.ActivationFunctionType
ALU = mybir.AluOpType
AXX = mybir.AxisListType.X

NCORES = 8


def full_cfg():
    return dict(D=4096, T=1024, FI=11008, TTW=512)


def small_cfg():
    return dict(D=512, T=256, FI=1280, TTW=128)


class Cfg:
    def __init__(self, d):
        self.D = d["D"]
        self.T = d["T"]
        self.FI = d["FI"]
        self.TTW = d["TTW"]
        self.DC = self.D // 128
        self.S = NCORES * self.T
        self.H = self.D // 128
        self.HD = 128
        self.NTT = self.T // self.TTW
        self.KC = self.S // 128
        self.FIC = self.FI // 128
        self.FJ = self.D // 2
        self.FJC = self.FJ // 128
        self.G = self.D // 64
        self.FJD = self.FI // 2
        self.DJC = self.FJD // 128
        self.DG = self.FI // 64
        self.DJH = (self.DJC + 1) // 2
        self.EPS_LN = 1e-5
        self.EPS_RMS = 1e-6
        self.DG_T0 = min(self.DG, 128)
        self.DG_T1 = self.DG - self.DG_T0
        self.WSUB = 4
        assert self.DC % self.WSUB == 0
        assert self.D % 128 == 0 and self.FI % 256 == 0
        assert self.T % self.TTW == 0 and self.TTW <= 512
        assert self.TTW % 128 == 0
        assert self.G <= 64 or self.G == 64, "gate/up groups must fit 64"


class Sem:
    __slots__ = ("h", "n")

    def __init__(self, h):
        self.h = h
        self.n = 0


class Emitter:
    ENGS = ("sync", "tensor", "scalar", "vector", "gpsimd")

    def __init__(self, nc):
        self.nc = nc
        self.role = None
        self.sems = []

    def sem(self, name):
        s = Sem(self.nc.alloc_semaphore(name=name))
        self.sems.append(s)
        return s

    def reset(self):
        for s in self.sems:
            s.n = 0

    def op(self, eng, fname, *args, inc=None, **kw):
        inst = None
        if self.role == eng:
            inst = getattr(getattr(self.nc, eng), fname)(*args, **kw)
            if inc:
                for s, v in inc:
                    inst.then_inc(s.h, v)
        if inc:
            for s, v in inc:
                s.n += v
        return inst

    def wait(self, eng, s, val):
        if self.role == eng:
            getattr(self.nc, eng).wait_ge(s.h, val)

    def mm(self, out, lhsT, rhs, start, stop, inc=None):
        self.op("tensor", "matmul", out, lhsT, rhs, start=start, stop=stop,
                inc=inc)

    def dma(self, eng, out, in_, inc=None, **kw):
        self.op(eng, "dma_start", out=out, in_=in_, inc=inc, **kw)


def build_program(cfg_dict):
    c = Cfg(cfg_dict)
    nc = bass.Bass("TRN2", target_bir_lowering=False, debug=False)
    E = Emitter(nc)
    RSQ = 1.0 / math.sqrt(c.HD)
    CS_OFF = 25.0

    def din(name, shape, dt):
        return nc.dram_tensor(name, shape, dt, kind="ExternalInput")

    def dint(name, shape, dt):
        return nc.dram_tensor(name, shape, dt, kind="Internal")

    xT = din("xT", [c.D, c.T], F32)
    wq = din("wq", [c.DC, 128, c.DC, 128], F32)
    wk = din("wk", [c.DC, 128, c.DC, 128], F32)
    wv = din("wv", [c.DC, 128, c.DC, 128], F32)
    wo = din("wo", [c.DC, 128, c.DC, 128], F32)
    bq = din("bq", [128, c.DC], F32)
    bk = din("bk", [128, c.DC], F32)
    bv = din("bv", [128, c.DC], F32)
    bo = din("bo", [128, c.DC], F32)
    gq = din("gq", [128, c.DC], F32)
    gk = din("gk", [128, c.DC], F32)
    wgp = din("wgp", [c.FIC, 128, c.FJC, 128], I32)
    wup = din("wup", [c.FIC, 128, c.FJC, 128], I32)
    wdp = din("wdp", [c.DC, 128, c.DJC, 128], I32)
    sg = din("sg", [c.G, c.FI], F32)
    zg = din("zg", [c.G, c.FI], F32)
    su = din("su", [c.G, c.FI], F32)
    zu = din("zu", [c.G, c.FI], F32)
    sd = din("sd", [c.DG, c.D], F32)
    zd = din("zd", [c.DG, c.D], F32)
    outT = nc.dram_tensor("outT", [c.D, c.T], F32, kind="ExternalOutput")

    wqb = dint("wqb", [c.DC, 128, c.DC, 128], BF16)
    wkb = dint("wkb", [c.DC, 128, c.DC, 128], BF16)
    wvb = dint("wvb", [c.DC, 128, c.DC, 128], BF16)
    wob = dint("wob", [c.DC, 128, c.DC, 128], BF16)
    kc_d = dint("kc_d", [c.D + 1, c.T], BF16)
    vc_d = dint("vc_d", [c.H * c.T, c.HD], BF16)
    kall = dint("kall", [NCORES * (c.D + 1), c.T], BF16)
    vall = dint("vall", [NCORES * c.H * c.T, c.HD], BF16)
    rbounce = dint("rbounce", [8, max(c.T, 128)], F32)
    xmid_d = dint("xmid_d", [c.D, c.T], F32)
    x2_d = dint("x2_d", [c.D, c.T], BF16)
    h_d = dint("h_d", [c.FI, c.T], BF16)
    ffn_d = dint("ffn_d", [c.D, c.T], BF16)
    sgbf_d = dint("sgbf_d", [c.G, c.FI], BF16)
    subf_d = dint("subf_d", [c.G, c.FI], BF16)
    wdeq_d = dint("wdeq_d", [c.DC, c.FI, 128], BF16)

    ctx = contextlib.ExitStack()

    def sbuf(name, shape, dt):
        return ctx.enter_context(nc.sbuf_tensor(name, shape, dt))

    NPT = 4
    A_N = max(c.DC * c.T,
              4 * c.KC * 128 + NPT * c.T + 2 * c.T,
              2 * c.FJC * c.T + 6 * c.T + 2 * c.FJC * 128,
              2 * c.DJC * c.TTW)
    B_N = max(c.DC * c.T, 2 * c.DC * c.TTW,
              16 * c.FJC * 128,
              11 * c.DJH * 128,
              3 * (max(c.G * c.FI, c.DG * c.D) // 128))
    A = sbuf("arenaA", [128, A_N], BF16)
    B = sbuf("arenaB", [128, B_N], BF16)

    WBW = (c.DC // c.WSUB) * 128
    wsb2 = [sbuf(f"wsb2_{i}", [128, max(WBW, 1024)], BF16) for i in range(4)]

    rowbuf = sbuf("rowbuf", [128, 4160], F32)
    rbc = [sbuf(f"rbc{i}", [128, c.TTW], F32) for i in range(2)]
    mbc = [sbuf(f"mbc{i}", [128, c.TTW], F32) for i in range(2)]

    bias_cols = sbuf("bias_cols", [128, max(c.H, 8)], F32)
    bqc = sbuf("bqc", [128, c.DC], F32)
    bkc = sbuf("bkc", [128, c.DC], F32)
    bvc = sbuf("bvc", [128, c.DC], F32)
    boc = sbuf("boc", [128, c.DC], F32)
    gqc = sbuf("gqc", [128, c.DC], F32)
    gkc = sbuf("gkc", [128, c.DC], F32)
    ident_bf = sbuf("ident_bf", [128, 128], BF16)
    Gpat = sbuf("Gpat", [128, 256], BF16)
    ones_bf = sbuf("ones_bf", [128, 1], BF16)
    ones_f = sbuf("ones_f", [128, 1], F32)
    ccol = sbuf("ccol", [128, 4], F32)
    xstg = [sbuf(f"xstg{i}", [128, c.TTW], F32) for i in range(2)]
    sqstg = [sbuf(f"sqstg{i}", [128, c.TTW], BF16) for i in range(2)]
    f32stg = [sbuf(f"f32stg{i}", [128, c.TTW], F32) for i in range(4)]
    bfstg = [sbuf(f"bfstg{i}", [128, c.TTW], BF16) for i in range(4)]
    G1 = min(c.G, 128)
    DG1 = max(c.DG_T1, 1)
    assert c.DG_T1 <= 64 and c.G <= 64
    w0f = wsb2[0].ap().bitcast(F32)
    w1f = wsb2[1].ap().bitcast(F32)
    szs = [w0f[0:c.G, 128 * i:128 * (i + 1)] for i in range(4)]
    szsd = [w1f[0:c.DG_T0, 128 * i:128 * (i + 1)] for i in range(4)]
    w2b = wsb2[2]
    szn_g = [w2b[0:c.G, 128 * i:128 * (i + 1)] for i in range(2)]
    szn_u = [w2b[0:c.G, 128 * (2 + i):128 * (3 + i)] for i in range(2)]
    sznd0 = [w2b[0:c.DG_T0, 128 * (4 + i):128 * (5 + i)] for i in range(2)]
    sznd1 = [w2b[0:DG1, 128 * (6 + i):128 * (7 + i)] for i in range(2)]
    xh_bf1 = wsb2[3][0:DG1, 0:c.T]
    xsum_x = rbc[1].ap().bitcast(BF16)[0:G1, 0:c.T]
    xh_bf0 = mbc[1].ap().bitcast(BF16)[0:c.DG_T0, 0:c.T]

    PS = ctx.enter_context(nc.psum_tensor("PS", [128, 4096], F32))

    def bank(i, p0=0, p1=128, o=0, w=512):
        return PS[p0:p1, i * 512 + o:i * 512 + o + w]

    def bankT(i):
        return PS[:, i * 512:i * 512 + c.TTW]

    def bank_bf(i, o=0, w=1024):
        return PS.ap().bitcast(BF16)[:, i * 1024 + o:i * 1024 + o + w]

    def Asl(off, n, p0=0, p1=128):
        return A[p0:p1, off:off + n]

    def Bsl(off, n, p0=0, p1=128):
        return B[p0:p1, off:off + n]

    def Bi32(off_i32, n_i32, p0=0, p1=128):
        return B.ap().bitcast(I32)[p0:p1, off_i32:off_i32 + n_i32]

    def Af32(off_f32, n_f32, p0=0, p1=128):
        return A.ap().bitcast(F32)[p0:p1, off_f32:off_f32 + n_f32]

    def Ai32(off_i32, n_i32, p0=0, p1=128):
        return A.ap().bitcast(I32)[p0:p1, off_i32:off_i32 + n_i32]

    FJW = c.FJC * 128
    _o = [0]

    def _alloc_b(n):
        off = _o[0]
        _o[0] += n
        assert _o[0] <= B_N, f"B overflow {_o[0]}>{B_N}"
        return off

    b_pk_g = _alloc_b(2 * FJW)
    b_pk_u = _alloc_b(2 * FJW)
    b_sx_g = _alloc_b(FJW)
    b_sx_u = _alloc_b(FJW)
    b_hi_g = [_alloc_b(FJW) for _ in range(2)]
    b_lo_g = [_alloc_b(FJW) for _ in range(2)]
    b_hi_u = [_alloc_b(FJW) for _ in range(2)]
    b_lo_u = [_alloc_b(FJW) for _ in range(2)]
    b_tmp6 = _alloc_b(FJW)
    b_tmp6b = _alloc_b(FJW)
    B_P6_END = _o[0]
    assert B_P6_END <= B_N

    DJW = c.DJH * 128
    _o[0] = 0
    b_pk_d = _alloc_b(2 * DJW)
    b_sx_d = [_alloc_b(DJW) for _ in range(2)]
    b_hi_d = [_alloc_b(DJW) for _ in range(3)]
    b_lo_d = [_alloc_b(DJW) for _ in range(3)]
    b_tmp7 = _alloc_b(DJW)
    B_P7_END = _o[0]
    assert B_P7_END <= B_N

    SCW = max(c.G * c.FI, c.DG * c.D) // 128
    assert 3 * SCW <= B_N, "scale staging must fit in B"

    A_xe = 0
    A_xo = c.FJC * c.T
    A_xh_f32 = (2 * c.FJC * c.T) // 2
    b_hstg = [2 * c.FJC * c.T + 4 * c.T + i * c.T for i in range(2)]
    A_sx2 = 2 * c.FJC * c.T + 6 * c.T
    assert A_sx2 + 2 * c.FJC * 128 <= A_N
    WDQ = min(11, c.DJC)
    A_wd = c.DC * c.T
    WDN = WDQ * 128
    assert A_wd + 2 * WDN + 2 * WDN + WDN + WDN + WDN <= A_N
    A_kb = [0, c.KC * 128]
    A_vb = [2 * c.KC * 128, 3 * c.KC * 128]
    A_pt = 4 * c.KC * 128
    A_he = 0
    A_ho = c.DJC * c.TTW

    sWL = E.sem("sWL")
    sWF = E.sem("sWF")
    sWB = E.sem("sWB")
    sCV = E.sem("sCV")
    sGD = E.sem("sGD")
    sSY = E.sem("sSY")
    sTM = E.sem("sTM")
    sVE = E.sem("sVE")
    sAC = E.sem("sAC")
    sPO = E.sem("sPO")
    sEX = E.sem("sEX")
    sSM = E.sem("sSM")
    sPVs = E.sem("sPVs")
    sCC = E.sem("sCC")
    sOUT = E.sem("sOUT")
    sBC = E.sem("sBC")
    sKV = E.sem("sKV")
    sWLs = [E.sem(f"sWL{i}") for i in range(4)]
    sWQ = [[E.sem(f"sWQ{i}_{q}") for q in range(4)] for i in range(2)]
    sWDL = E.sem("sWDL")
    sP0L = [E.sem(f"sP0L{i}") for i in range(6)]
    sP0S = [E.sem(f"sP0S{i}") for i in range(6)]
    sP0Cv = E.sem("sP0Cv")
    sP0Cs = E.sem("sP0Cs")
    sPKH = [E.sem(f"sPKH{i}") for i in range(2)]
    sSXL = [E.sem(f"sSXL{i}") for i in range(2)]
    sXSs = [E.sem(f"sXS{i}") for i in range(2)]
    sBFs = [E.sem(f"sBF{i}") for i in range(4)]
    sHSs = [E.sem(f"sHS{i}") for i in range(2)]
    sVCs = [E.sem(f"sVC{i}") for i in range(2)]
    sWD = E.sem("sWD")

    ROWS = {}
    _row_off = [0]

    def row_reset():
        ROWS.clear()
        _row_off[0] = 0

    def row(name, n):
        if name not in ROWS:
            off = _row_off[0]
            _row_off[0] += n
            assert _row_off[0] <= 4160, "rowbuf overflow"
            ROWS[name] = (off, n)
        off, nn = ROWS[name]
        assert nn == n
        return rowbuf[0:1, off:off + n]

    def row_seg(name, seg_off, w):
        off, nn = ROWS[name]
        assert seg_off + w <= nn
        return rowbuf[0:1, off + seg_off:off + seg_off + w]

    marks = {}

    def prologue():
        g = "gpsimd"
        E.op(g, "memset", ident_bf[:], 0.0, inc=[(sPO, 1)])
        E.wait(g, sPO, sPO.n)
        E.op(g, "affine_select", out=ident_bf[:], in_=ident_bf[:],
             compare_op=ALU.not_equal, fill=1.0, base=0,
             pattern=[[-1, 128]], channel_multiplier=1, inc=[(sPO, 1)])
        E.op(g, "memset", Gpat[:], 0.0, inc=[(sPO, 1)])
        E.wait(g, sPO, sPO.n)
        E.op(g, "memset", Gpat[0:64, 128:129], 1.0, inc=[(sPO, 1)])
        E.op(g, "memset", Gpat[64:128, 129:130], 1.0, inc=[(sPO, 1)])
        E.op(g, "memset", ones_bf[:], 1.0, inc=[(sPO, 1)])
        E.op(g, "memset", ones_f[:], 1.0, inc=[(sPO, 1)])
        E.op(g, "memset", ccol[:, 0:1], float(c.EPS_LN), inc=[(sPO, 1)])
        E.op(g, "memset", ccol[:, 1:2], float(c.EPS_RMS), inc=[(sPO, 1)])
        E.op(g, "memset", ccol[:, 2:3], 1e-30, inc=[(sPO, 1)])
        E.op(g, "memset", ccol[:, 3:4], -0.5 * math.log(c.HD),
             inc=[(sPO, 1)])
        for tdst, src in ((bqc, bq), (bkc, bk), (bvc, bv), (boc, bo),
                          (gqc, gq), (gkc, gk)):
            E.dma(g, tdst[:], src.ap(), inc=[(sGD, 16)])
        for eng in Emitter.ENGS:
            E.wait(eng, sPO, sPO.n)
            E.wait(eng, sGD, sGD.n)

    def row_rstd(tag, sum_ap, sum2_ap, n, eps, with_mean=True):
        eps_col = {c.EPS_LN: 0, c.EPS_RMS: 1}[eps]
        E.wait("vector", sAC, sAC.n)
        E.wait("vector", sBC, sBC.n)
        v, s = "vector", "scalar"
        w = sum2_ap.shape[-1]
        ex2 = row("sh_ex2", w)
        r_r = row(tag + "_r", w)
        E.op(v, "tensor_scalar", out=ex2, in0=sum2_ap, scalar1=1.0 / n,
             scalar2=None, op0=ALU.mult, inc=[(sVE, 1)])
        if with_mean:
            mu = row("sh_mu", w)
            tsq = row("sh_tsq", w)
            m2_r = row(tag + "_m2", w)
            E.op(v, "tensor_scalar", out=mu, in0=sum_ap, scalar1=1.0 / n,
                 scalar2=None, op0=ALU.mult, inc=[(sVE, 1)])
            E.wait(v, sVE, sVE.n)
            E.op(v, "tensor_tensor", out=tsq, in0=mu, in1=mu, op=ALU.mult,
                 inc=[(sVE, 1)])
            E.wait(v, sVE, sVE.n)
            E.op(v, "tensor_tensor", out=ex2, in0=ex2, in1=tsq,
                 op=ALU.subtract, inc=[(sVE, 1)])
        lnv = row("sh_lnv", w)
        E.wait(s, sVE, sVE.n)
        E.op(s, "activation", lnv, ex2, AF.Ln,
             bias=ccol[0:1, eps_col:eps_col + 1], scale=1.0,
             inc=[(sAC, 1)])
        E.wait(s, sAC, sAC.n)
        E.op(s, "activation", r_r, lnv, AF.Exp, bias=0.0, scale=-0.5,
             inc=[(sAC, 1)])
        if with_mean:
            E.wait(v, sAC, sAC.n)
            E.op(v, "scalar_tensor_tensor", out=m2_r, in0=mu, scalar=-1.0,
                 in1=r_r, op0=ALU.mult, op1=ALU.mult, inc=[(sVE, 1)])
            return r_r, m2_r
        return r_r, None

    def bounce_rows(pairs, scr_row):
        g = "gpsimd"
        E.wait(g, sBC, sBC.n)
        E.wait(g, sVE, sVE.n)
        E.wait(g, sAC, sAC.n)
        for i, (rap, _) in enumerate(pairs):
            w = rap.shape[-1]
            E.dma(g, rbounce.ap()[scr_row + i:scr_row + i + 1, 0:w], rap,
                  inc=[(sBC, 16)])
        E.wait(g, sBC, sBC.n)
        for i, (rap, tap) in enumerate(pairs):
            w = rap.shape[-1]
            src = AP(tensor=rbounce.ap().tensor,
                     offset=(scr_row + i) * rbounce.shape[1],
                     ap=[[0, 128], [1, w]])
            E.dma(g, tap, src, inc=[(sBC, 16)])
        return sBC.n

    def emit_nibbles(pk_i32, hi_bf, lo_bf, tmp_i32, W, sx_bf=None,
                     eng="vector", hi_shift=False):
        v = eng
        sem = sPO if eng == "gpsimd" else sVE
        HW = W // 2
        for hf2 in range(2):
            a, b2 = hf2 * HW, (hf2 + 1) * HW
            for mask, dst in ((0xF0, hi_bf), (0x0F, lo_bf)):
                E.wait(v, sem, sem.n)
                if mask == 0xF0 and hi_shift:
                    E.op(v, "tensor_scalar", out=tmp_i32[:, 0:HW],
                         in0=pk_i32[:, a:b2], scalar1=0xF0, scalar2=4,
                         op0=ALU.bitwise_and,
                         op1=ALU.logical_shift_right, inc=[(sem, 1)])
                else:
                    E.op(v, "tensor_scalar", out=tmp_i32[:, 0:HW],
                         in0=pk_i32[:, a:b2], scalar1=mask, scalar2=None,
                         op0=ALU.bitwise_and, inc=[(sem, 1)])
                E.wait(v, sem, sem.n)
                if sx_bf is None:
                    E.op(v, "tensor_copy", dst[:, a:b2], tmp_i32[:, 0:HW],
                         inc=[(sem, 1)])
                else:
                    E.op(v, "tensor_tensor", out=dst[:, a:b2],
                         in0=tmp_i32[:, 0:HW], in1=sx_bf[:, a:b2],
                         op=ALU.mult, inc=[(sem, 1)])
        E.wait(v, sem, sem.n)

    wd_state = {"i": 0, "nib": {}, "ld": {}, "st": {}}

    def wd_steps_total():
        out = []
        for hc in range(c.DC):
            j0 = 0
            while j0 < c.DJC:
                j1 = min(c.DJC, j0 + WDQ)
                out.append((hc, j0, j1))
                j0 = j1
        return out

    WD_STEPS = wd_steps_total()

    def do_wd_step():
        k = wd_state["i"]
        if k >= len(WD_STEPS):
            return
        wd_state["i"] = k + 1
        hc, j0, j1 = WD_STEPS[k]
        nw = (j1 - j0) * 128
        v = "vector"
        pk = Ai32(A_wd // 2, WDN)
        sx = Af32((A_wd + 2 * WDN) // 2, WDN)
        hi = Asl(A_wd + 4 * WDN, WDN)
        lo = Asl(A_wd + 5 * WDN, WDN)
        tmp = Ai32((A_wd + 6 * WDN) // 2, WDN // 2)
        if k >= 1:
            E.wait("sync", sVE, wd_state["nib"][k - 1])
        E.dma("sync", pk[:, 0:nw].rearrange("p (jc i) -> p jc i", i=128),
              wdp.ap()[hc, :, j0:j1, :], inc=[(sWDL, 16)])
        for b in range(4):
            s2 = AP(tensor=sd.ap().tensor,
                    offset=(4 * j0 + b) * c.D + hc * 128,
                    ap=[[0, 32], [4 * c.D, j1 - j0], [1, 128]])
            E.dma("sync", sx[32 * b:32 * b + 32, 0:nw].rearrange(
                "p (jc i) -> p jc i", i=128), s2, inc=[(sWDL, 16)])
        ldm = sWDL.n
        E.wait(v, sWDL, ldm)
        if k >= 1:
            E.wait(v, sWD, wd_state["st"][k - 1])
        emit_nibbles(pk[:, 0:nw], hi[:, 0:nw], lo[:, 0:nw], tmp, nw,
                     sx_bf=sx, hi_shift=True)
        wd_state["nib"][k] = sVE.n
        g2 = "gpsimd"
        E.wait(g2, sVE, wd_state["nib"][k])
        for par, slab in ((0, hi), (1, lo)):
            dst = AP(tensor=wdeq_d.ap().tensor,
                     offset=(hc * c.FI + 2 * j0 * 128 + par) * 128,
                     ap=[[2 * 128, 128], [2 * 128 * 128, j1 - j0],
                         [1, 128]])
            E.dma(g2, dst, slab[:, 0:nw].rearrange(
                "p (jc i) -> p jc i", i=128), inc=[(sWD, 16)])
        wd_state["st"][k] = sWD.n

    P0_TW = min(512, c.TTW)
    P0_SRC = [(wv, wvb), (wk, wkb), (wq, wqb), (wo, wob)]
    P0_TPM = (c.D * c.D) // (128 * P0_TW)
    P0_TOT = 4 * P0_TPM
    p0_state = {"j": 0, "pend": [], "final": None}
    p0_slot_in = [wsb2[0], wsb2[1], wsb2[2], wsb2[3], rbc[1], mbc[1]]
    p0_slot_out = [sqstg[0], sqstg[1], bfstg[0], bfstg[1], bfstg[2],
                   bfstg[3]]

    def p0_in_ap(k):
        t_ = p0_slot_in[k]
        if k < 4:
            return t_.ap().bitcast(F32)[:, 0:P0_TW]
        return t_[:, 0:P0_TW]

    def p0_emit_store(j):
        m, r = j // P0_TPM, j % P0_TPM
        ksl = j % 6
        csem = sP0Cv if j % 2 == 0 else sP0Cs
        E.wait("sync", csem, j // 2 + 1)
        dst = AP(tensor=P0_SRC[m][1].ap().tensor, offset=r * 128 * P0_TW,
                 ap=[[P0_TW, 128], [1, P0_TW]])
        E.dma("sync", dst, p0_slot_out[ksl][:, 0:P0_TW],
              inc=[(sP0S[ksl], 16)])

    def do_p0_step():
        j = p0_state["j"]
        if j >= P0_TOT:
            return
        p0_state["j"] = j + 1
        m, r = j // P0_TPM, j % P0_TPM
        ksl = j % 6
        eng = "vector" if j % 2 == 0 else "scalar"
        csem = sP0Cv if j % 2 == 0 else sP0Cs
        if j >= 6:
            E.wait("sync", csem, j // 2 - 2)
        srcap = AP(tensor=P0_SRC[m][0].ap().tensor, offset=r * 128 * P0_TW,
                   ap=[[P0_TW, 128], [1, P0_TW]])
        E.dma("sync", p0_in_ap(ksl), srcap, inc=[(sP0L[ksl], 16)])
        E.wait(eng, sP0L[ksl], sP0L[ksl].n)
        if j >= 6:
            E.wait(eng, sP0S[ksl], sP0S[ksl].n)
        if eng == "vector":
            E.op(eng, "tensor_copy", p0_slot_out[ksl][:, 0:P0_TW],
                 p0_in_ap(ksl), inc=[(csem, 1)])
        else:
            E.op(eng, "activation", p0_slot_out[ksl][:, 0:P0_TW],
                 p0_in_ap(ksl), AF.Copy, bias=0.0, scale=1.0,
                 inc=[(csem, 1)])
        p0_state["pend"].append(j)
        if len(p0_state["pend"]) > 4:
            p0_emit_store(p0_state["pend"].pop(0))

    def p0_drain():
        while p0_state["j"] < P0_TOT:
            do_p0_step()
        while p0_state["pend"]:
            p0_emit_store(p0_state["pend"].pop(0))
        p0_state["final"] = [sm.n for sm in sP0S]

    wblk = [0]

    def stream_w_subblock(src_ap):
        idx = wblk[0]
        wblk[0] += 1
        sl = idx % 4
        if idx == 0:
            for k in range(6):
                E.wait("sync", sP0S[k], p0_state["final"][k])
        if idx >= 4:
            E.wait("sync", sWB, idx - 3)
        E.dma("sync", wsb2[sl][:, 0:WBW].rearrange("p (h i) -> p h i", i=128),
              src_ap, inc=[(sWLs[sl], 16)])
        return sl, sWLs[sl].n

    def p1_ln1():
        v, t, g = "vector", "tensor", "gpsimd"
        row_reset()
        xf = B.ap().bitcast(F32)
        for tt in range(c.NTT):
            t0 = tt * c.TTW
            stat_marks = {}
            for ic in range(c.DC):
                E.dma(g, xf[:, ic * c.TTW:(ic + 1) * c.TTW],
                      xT.ap()[ic * 128:(ic + 1) * 128, t0:t0 + c.TTW],
                      inc=[(sGD, 16)])
            tile_ld = sGD.n
            for ic in range(c.DC):
                sl = ic % 2
                E.wait(v, sGD, tile_ld)
                if ic >= 2:
                    E.wait(v, sTM, stat_marks[ic - 2])
                xc = xf[:, ic * c.TTW:(ic + 1) * c.TTW]
                E.op(v, "tensor_tensor", out=f32stg[sl][:], in0=xc, in1=xc,
                     op=ALU.mult, inc=[(sVE, 1)])
                E.wait(t, sGD, tile_ld)
                E.wait(t, sVE, sVE.n)
                E.mm(bank(6, 0, 1, 0, c.TTW), ones_f[:], xc,
                     start=(ic == 0), stop=(ic == c.DC - 1))
                E.mm(bank(7, 0, 1, 0, c.TTW), ones_f[:], f32stg[sl][:],
                     start=(ic == 0), stop=(ic == c.DC - 1), inc=[(sTM, 1)])
                stat_marks[ic] = sTM.n
            E.wait(v, sTM, sTM.n)
            sum_r = row("p1_sum", c.TTW)
            sum2_r = row("p1_sum2", c.TTW)
            E.op(v, "tensor_copy", sum_r, bank(6, 0, 1, 0, c.TTW),
                 inc=[(sVE, 1)])
            E.op(v, "tensor_copy", sum2_r, bank(7, 0, 1, 0, c.TTW),
                 inc=[(sVE, 1)])
            E.wait(v, sVE, sVE.n)
            r_r, m2_r = row_rstd("p1", sum_r, sum2_r, c.D, c.EPS_LN)
            bc_mark = bounce_rows([(r_r, rbc[0][:]), (m2_r, mbc[0][:])], 0)
            for ic in range(c.DC):
                for _ in range(16):
                    do_p0_step()
                sl = 2 + ic % 2
                xc = xf[:, ic * c.TTW:(ic + 1) * c.TTW]
                E.wait(v, sBC, bc_mark)
                E.op(v, "tensor_tensor", out=f32stg[sl][:], in0=xc,
                     in1=rbc[0][:], op=ALU.mult, inc=[(sVE, 1)])
                E.wait(v, sVE, sVE.n)
                x1c = Asl(ic * c.T + t0, c.TTW)
                E.op(v, "tensor_tensor", out=x1c, in0=f32stg[sl][:],
                     in1=mbc[0][:], op=ALU.add, inc=[(sVE, 1)])
            E.wait(g, sVE, sVE.n)
        marks["x1"] = sVE.n

    def qkv_matrix_mms(w_dram, ic, banks, extra_waits):
        t = "tensor"
        hpw = c.DC // c.WSUB
        for sub in range(c.WSUB):
            src_ap = w_dram.ap()[ic][:, sub * hpw:(sub + 1) * hpw, :]
            sl, lm = stream_w_subblock(src_ap)
            E.wait(t, sWLs[sl], lm)
            if sub == 0:
                for wv_ in extra_waits:
                    E.wait(t, sVE, wv_)
            for hh in range(hpw):
                hc = sub * hpw + hh
                last_h = (sub == c.WSUB - 1 and hh == hpw - 1)
                for tt in range(c.NTT):
                    t0 = tt * c.TTW
                    sub_last = (hh == hpw - 1 and tt == c.NTT - 1)
                    E.mm(bankT(banks[tt]),
                         wsb2[sl][:, hh * 128:(hh + 1) * 128],
                         Asl(hc * c.T + t0, c.TTW),
                         start=(hc == 0), stop=last_h,
                         inc=[(sWB, 1)] if sub_last else None)
        return sWB.n

    def qk_matrix(w_dram, b_col, g_col, name, x1_wait=None):
        v, t = "vector", "tensor"
        row_reset()
        ev_marks = {}
        sq_marks = {}
        sq_acm = {}

        def emit_sq(gj):
            ttj = gj % 2
            E.wait(t, sAC, sq_acm[gj])
            E.mm(bank(6 + ttj, 0, 1, 0, c.TTW), ones_bf[:],
                 sqstg[gj % 2][:], start=(gj < 2),
                 stop=(gj >= 2 * (c.DC - 1)), inc=[(sTM, 1)])
            sq_marks[gj] = sTM.n

        for ic in range(c.DC):
            do_wd_step()
            banks = [(2 * ic + tt) % 4 for tt in range(c.NTT)]
            waits = list(x1_wait) if (ic == 0 and x1_wait) else []
            for tt in range(c.NTT):
                gi = 2 * ic + tt
                if gi >= 4:
                    waits.append(ev_marks[gi - 4])
            mmk = qkv_matrix_mms(w_dram, ic, banks, waits)
            for tt in range(c.NTT):
                t0 = tt * c.TTW
                gi = 2 * ic + tt
                if gi >= 2:
                    emit_sq(gi - 2)
                E.wait(v, sWB, mmk)
                pc = Bsl(ic * c.T + t0, c.TTW)
                E.op(v, "tensor_scalar", out=pc, in0=bankT(banks[tt]),
                     scalar1=b_col[:, ic:ic + 1], scalar2=None,
                     op0=ALU.add, inc=[(sVE, 1)])
                ev_marks[gi] = sVE.n
                ssl = gi % 2
                E.wait("scalar", sVE, ev_marks[gi])
                if gi >= 2:
                    E.wait("scalar", sTM, sq_marks[gi - 2])
                E.op("scalar", "activation", sqstg[ssl][:], pc, AF.Square,
                     bias=0.0, scale=1.0, inc=[(sAC, 1)])
                sq_acm[gi] = sAC.n
        emit_sq(2 * c.DC - 2)
        emit_sq(2 * c.DC - 1)
        E.wait(v, sTM, sTM.n)
        rtiles = []
        for tt in range(c.NTT):
            sum2_r = row(f"{name}_s2_{tt}", c.TTW)
            E.op(v, "tensor_copy", sum2_r, bank(6 + tt, 0, 1, 0, c.TTW),
                 inc=[(sVE, 1)])
            E.wait(v, sVE, sVE.n)
            r_r, _ = row_rstd(f"{name}_{tt}", None, sum2_r, c.D, c.EPS_RMS,
                              with_mean=False)
            rtiles.append((r_r, rbc[tt][:]))
        bc_mark = bounce_rows(rtiles, 2)
        for ic in range(c.DC):
            for tt in range(c.NTT):
                t0 = tt * c.TTW
                pc = Bsl(ic * c.T + t0, c.TTW)
                E.wait(v, sBC, bc_mark)
                E.op(v, "scalar_tensor_tensor", out=pc, in0=pc,
                     scalar=g_col[:, ic:ic + 1], in1=rbc[tt][:],
                     op0=ALU.mult, op1=ALU.mult, inc=[(sVE, 1)])
        return sVE.n

    def p2_qkv():
        v, s, t, g = "vector", "scalar", "tensor", "gpsimd"
        for k2 in (0, 1):
            E.wait(s, sP0S[k2], p0_state["final"][k2])
        for k2 in (2, 3, 4, 5):
            E.wait(v, sP0S[k2], p0_state["final"][k2])
        kmark = qk_matrix(wkb, bkc, gkc, "p2k",
                          x1_wait=[marks["x1"]])
        E.wait(v, sVE, kmark)
        E.wait(g, sVE, kmark)
        for ic in range(c.DC):
            E.dma(g, kc_d.ap()[ic * 128:(ic + 1) * 128, :],
                  Bsl(ic * c.T, c.T), inc=[(sGD, 16)])
        knloc = row("knloc", max(c.H, 8))
        for h in range(c.H):
            for tt in range(c.NTT):
                ssl = tt % 2
                E.wait(v, sTM, sTM.n)
                kc_ap = Bsl(h * c.T + tt * c.TTW, c.TTW)
                E.op(v, "tensor_tensor", out=sqstg[ssl][:], in0=kc_ap,
                     in1=kc_ap, op=ALU.mult, inc=[(sVE, 1)])
                E.wait(t, sVE, sVE.n)
                E.mm(bank(4 + tt, 0, 1, 0, c.TTW), ones_bf[:], sqstg[ssl][:],
                     start=True, stop=True, inc=[(sTM, 1)])
            E.wait(v, sTM, sTM.n)
            knr = row("knr", c.T)
            for tt in range(c.NTT):
                E.op(v, "tensor_copy", row_seg("knr", tt * c.TTW, c.TTW),
                     bank(4 + tt, 0, 1, 0, c.TTW), inc=[(sVE, 1)])
            E.wait(v, sVE, sVE.n)
            E.op(v, "reduce_max", knloc[0:1, h:h + 1], knr, axis=AXX,
                 inc=[(sVE, 1)])
            E.wait(v, sVE, sVE.n)
        E.op(v, "tensor_copy", sqstg[0][0:1, 0:c.H], knloc[0:1, 0:c.H],
             inc=[(sVE, 1)])
        E.op(v, "memset", sqstg[1][0:1, :], 0.0, inc=[(sVE, 1)])
        E.wait(g, sVE, sVE.n)
        E.dma(g, kc_d.ap()[c.D:c.D + 1, 0:c.H], sqstg[0][0:1, 0:c.H],
              inc=[(sGD, 16)])
        off = c.H
        while off < c.T:
            wseg = min(c.TTW, c.T - off)
            E.dma(g, kc_d.ap()[c.D:c.D + 1, off:off + wseg],
                  sqstg[1][0:1, 0:wseg], inc=[(sGD, 16)])
            off += wseg
        marks["contrib"] = sGD.n

        E.wait(g, sGD, marks["contrib"])
        E.op(g, "collective_compute", "AllGather", ALU.bypass,
             replica_groups=[list(range(NCORES))],
             ins=[kc_d.ap().opt()], outs=[kall.ap().opt()],
             inc=[(sCC, 1)])

        row_reset()
        tr_bank_mark = {2: None, 3: None}
        vev_marks = {}
        vst_marks = {}
        for ic in range(c.DC):
            do_wd_step()
            banks = [(2 * ic + tt) % 2 for tt in range(c.NTT)]
            waits = []
            for tt in range(c.NTT):
                gi = 2 * ic + tt
                if gi >= 2:
                    waits.append(vev_marks[gi - 2])
            mmk = qkv_matrix_mms(wvb, ic, banks, waits)
            for tt in range(c.NTT):
                t0 = tt * c.TTW
                gi = 2 * ic + tt
                E.wait(v, sWB, mmk)
                vp = bfstg[gi % 2][:]
                if gi >= 2:
                    E.wait(v, sTM, vst_marks[gi - 2])
                E.op(v, "tensor_scalar", out=vp, in0=bankT(banks[tt]),
                     scalar1=bvc[:, ic:ic + 1], scalar2=None, op0=ALU.add,
                     inc=[(sVE, 1)])
                vev_marks[gi] = sVE.n
                tb = 2 + gi % 2
                ntile = c.TTW // 128
                E.wait(t, sVE, sVE.n)
                if tr_bank_mark[tb] is not None:
                    E.wait(t, sVE, tr_bank_mark[tb])
                for tl in range(ntile):
                    E.op(t, "transpose", bank_bf(tb, tl * 128, 128),
                         vp[:, tl * 128:(tl + 1) * 128], ident_bf[:],
                         inc=[(sTM, 1)] if tl == ntile - 1 else None)
                vst_marks[gi] = sTM.n
                E.wait(v, sTM, sTM.n)
                if gi >= 2:
                    E.wait(v, sVCs[gi % 2], sVCs[gi % 2].n)
                vt = bfstg[2 + gi % 2][:]
                E.op(v, "tensor_copy", vt[:, 0:ntile * 128],
                     bank_bf(tb, 0, ntile * 128), inc=[(sVE, 1)])
                tr_bank_mark[tb] = sVE.n
                E.wait(g, sVE, sVE.n)
                for tl in range(ntile):
                    tc = t0 // 128 + tl
                    E.dma(g, vc_d.ap()[ic * c.T + tc * 128:
                                       ic * c.T + (tc + 1) * 128, :],
                          vt[:, tl * 128:(tl + 1) * 128],
                          inc=[(sVCs[gi % 2], 16)])

        marks["v_done"] = sVE.n
        E.wait(g, sVCs[0], sVCs[0].n)
        E.wait(g, sVCs[1], sVCs[1].n)
        E.op(g, "collective_compute", "AllGather", ALU.bypass,
             replica_groups=[list(range(NCORES))],
             ins=[vc_d.ap().opt()], outs=[vall.ap().opt()],
             inc=[(sCC, 1)])

        E.wait(v, sGD, marks["contrib"])
        marks["q_ready"] = qk_matrix(wqb, bqc, gqc, "p2q",
                                     x1_wait=[marks["v_done"]])

    def p3_collect():
        v, s, t, g = "vector", "scalar", "tensor", "gpsimd"
        row_reset()
        marks["cc"] = sCC.n
        marks["wd_done"] = sWD.n
        E.wait(g, sCC, sCC.n)
        E.wait(v, sVE, marks["q_ready"])
        mq2 = row("mq2", max(c.H, 8))
        for h in range(c.H):
            for tt in range(c.NTT):
                ssl = tt % 2
                E.wait(v, sTM, sTM.n)
                qc_ap = Bsl(h * c.T + tt * c.TTW, c.TTW)
                E.op(v, "tensor_tensor", out=sqstg[ssl][:], in0=qc_ap,
                     in1=qc_ap, op=ALU.mult, inc=[(sVE, 1)])
                E.wait(t, sVE, sVE.n)
                E.mm(bank(4 + tt, 0, 1, 0, c.TTW), ones_bf[:], sqstg[ssl][:],
                     start=True, stop=True, inc=[(sTM, 1)])
            E.wait(v, sTM, sTM.n)
            qnr = row("qnr", c.T)
            for tt in range(c.NTT):
                E.op(v, "tensor_copy", row_seg("qnr", tt * c.TTW, c.TTW),
                     bank(4 + tt, 0, 1, 0, c.TTW), inc=[(sVE, 1)])
            E.wait(v, sVE, sVE.n)
            E.op(v, "reduce_max", mq2[0:1, h:h + 1], qnr, axis=AXX,
                 inc=[(sVE, 1)])
            E.wait(v, sVE, sVE.n)
        E.wait(g, sVE, sVE.n)
        E.wait(g, sCC, marks["cc"])
        for h in range(c.H):
            src = AP(tensor=kall.ap().tensor, offset=c.D * c.T + h,
                     ap=[[(c.D + 1) * c.T, NCORES], [1, 1]])
            E.dma(g, sqstg[0][0:1, h * NCORES:(h + 1) * NCORES].rearrange(
                "p (r one) -> p r one", r=NCORES), src, inc=[(sGD, 16)],
                allow_slow_non_contiguous=True)
        E.wait(v, sGD, sGD.n)
        mk2 = row("mk2", max(c.H, 8))
        for h in range(c.H):
            E.op(v, "reduce_max", mk2[0:1, h:h + 1],
                 sqstg[0][0:1, h * NCORES:(h + 1) * NCORES], axis=AXX,
                 inc=[(sVE, 1)])
        E.wait(v, sVE, sVE.n)
        prod = row("bias_prod", max(c.H, 8))
        E.op(v, "tensor_tensor", out=prod, in0=mq2, in1=mk2, op=ALU.mult,
             inc=[(sVE, 1)])
        E.wait(s, sVE, sVE.n)
        lnp = row("bias_ln", max(c.H, 8))
        E.op(s, "activation", lnp, prod, AF.Ln, bias=ccol[0:1, 2:3],
             scale=1.0, inc=[(sAC, 1)])
        E.wait(s, sAC, sAC.n)
        cs_r = row("bias_cs", max(c.H, 8))
        E.op(s, "activation", cs_r, lnp, AF.Exp, bias=ccol[0:1, 3:4],
             scale=0.5, inc=[(sAC, 1)])
        E.wait(v, sAC, sAC.n)
        negc = row("bias_negc", max(c.H, 8))
        E.op(v, "tensor_scalar", out=negc, in0=cs_r, scalar1=-1.0,
             scalar2=CS_OFF, op0=ALU.mult, op1=ALU.add, inc=[(sVE, 1)])
        E.wait(g, sVE, sVE.n)
        E.wait(g, sBC, sBC.n)
        E.dma(g, rbounce.ap()[6:7, 0:max(c.H, 8)], negc, inc=[(sBC, 16)])
        E.wait(g, sBC, sBC.n)
        for h in range(c.H):
            src = AP(tensor=rbounce.ap().tensor,
                     offset=6 * rbounce.shape[1] + h,
                     ap=[[0, 128], [1, 1]])
            E.dma(g, bias_cols[:, h:h + 1], src, inc=[(sBC, 16)])
        marks["bias_cols"] = sBC.n

    def p4_attention():
        v, s, t, g = "vector", "scalar", "tensor", "gpsimd"
        row_reset()
        KC = c.KC
        A_den_f32 = (4 * c.KC * 128 + NPT * c.T) // 2
        den_acc = Af32(A_den_f32, c.T)
        kv_ld = {}
        pv_done_mark = {}
        oe_marks = {}
        exp_mark = {}
        pv_mark = {}
        den_mark = {}

        def load_head(h):
            slb = h % 2
            E.wait(g, sCC, marks["cc"])
            E.wait(g, sVE, marks["q_ready"])
            E.wait(g, sKV, sKV.n)
            if h >= 2:
                E.wait(g, sPVs, pv_done_mark[h - 2])
            for r in range(NCORES):
                E.dma(g, Asl(A_kb[slb] + r * c.T, c.T),
                      kall.ap()[r * (c.D + 1) + h * 128:
                                r * (c.D + 1) + (h + 1) * 128, 0:c.T],
                      inc=[(sKV, 16)])
            for r in range(NCORES):
                src = AP(tensor=vall.ap().tensor,
                         offset=(r * c.H * c.T + h * c.T) * c.HD,
                         ap=[[c.HD, 128], [128 * c.HD, c.T // 128],
                             [1, c.HD]])
                E.dma(g, Asl(A_vb[slb] + r * c.T, c.T).rearrange(
                    "p (cc d) -> p cc d", d=c.HD), src, inc=[(sKV, 16)])
            kv_ld[h] = sKV.n

        for eng in ("scalar", "tensor", "vector", "gpsimd"):
            E.wait(eng, sWD, marks["wd_done"])
        load_head(0)
        if c.H > 1:
            load_head(1)
        E.wait(s, sBC, marks["bias_cols"])
        for h in range(c.H):
            slb = h % 2
            E.wait(v, sVE, sVE.n)
            E.wait(v, sTM, sTM.n)
            E.op(v, "memset", den_acc, 0.0, inc=[(sVE, 1)])
            E.wait(t, sKV, kv_ld[h])
            E.wait(t, sVE, marks["q_ready"])
            if h >= 1:
                E.wait(t, sVE, oe_marks[h - 1])
            def emit_pv(kc):
                pslot = kc % NPT
                E.wait(t, sEX, exp_mark[(h, kc)])
                vch = Asl(A_vb[slb] + kc * 128, 128)
                E.mm(bank(6, 0, 1, 0, c.TTW), ones_bf[:],
                     Asl(A_pt + pslot * c.T, c.TTW),
                     start=(kc == 0), stop=(kc == KC - 1))
                for qt in range(c.NTT):
                    q0 = qt * c.TTW
                    E.mm(bank(4 + qt, 0, 128, 0, c.TTW), vch,
                         Asl(A_pt + pslot * c.T + q0, c.TTW),
                         start=(kc == 0), stop=(kc == KC - 1),
                         inc=[(sPVs, 1)] if qt == c.NTT - 1 else None)
                pv_mark[(h, kc)] = sPVs.n
                E.wait(v, sEX, exp_mark[(h, kc)])
                E.wait(v, sVE, sVE.n)
                E.op(v, "tensor_tensor",
                     out=den_acc[:, c.TTW:c.T], in0=den_acc[:, c.TTW:c.T],
                     in1=Asl(A_pt + pslot * c.T + c.TTW, c.T - c.TTW),
                     op=ALU.add, inc=[(sVE, 1)])
                den_mark[(h, kc)] = sVE.n

            for kc in range(KC):
                grp = kc % 2
                if (h, kc - 2) in exp_mark:
                    E.wait(t, sEX, exp_mark[(h, kc - 2)])
                elif kc < 2 and (h - 1, KC - 2 + kc) in exp_mark:
                    E.wait(t, sEX, exp_mark[(h - 1, KC - 2 + kc)])
                lhs = Asl(A_kb[slb] + kc * 128, 128)
                for qt in range(c.NTT):
                    q0 = qt * c.TTW
                    E.mm(bank(grp * 2 + q0 // 512, 0, 128, q0 % 512, c.TTW),
                         lhs, Bsl(h * c.T + q0, c.TTW),
                         start=True, stop=True,
                         inc=[(sSM, 1)] if qt == c.NTT - 1 else None)
                sm_mark = sSM.n
                pslot = kc % NPT
                E.wait(s, sSM, sm_mark)
                if (h, kc - NPT) in pv_mark:
                    E.wait(s, sPVs, pv_mark[(h, kc - NPT)])
                    E.wait(s, sVE, den_mark[(h, kc - NPT)])
                elif kc < NPT and (h - 1, KC - NPT + kc) in pv_mark:
                    E.wait(s, sPVs, pv_mark[(h - 1, KC - NPT + kc)])
                    E.wait(s, sVE, den_mark[(h - 1, KC - NPT + kc)])
                E.op(s, "activation", Asl(A_pt + pslot * c.T, c.T),
                     PS[:, grp * 1024:grp * 1024 + c.T], AF.Exp,
                     bias=bias_cols[:, h:h + 1], scale=RSQ, inc=[(sEX, 1)])
                exp_mark[(h, kc)] = sEX.n
                if kc >= 1:
                    emit_pv(kc - 1)
            emit_pv(KC - 1)
            pv_done_mark[h] = sPVs.n
            if h + 2 < c.H:
                load_head(h + 2)
            E.wait(v, sVE, den_mark[(h, KC - 1)])
            if h >= 1:
                E.wait(v, sVE, marks.get("nrm_%d" % (h - 1), 0))
            den_r = row("den_r", c.T)
            E.wait(t, sVE, sVE.n)
            for qt in range(1, c.NTT):
                E.mm(bank(6 + qt, 0, 1, 0, c.TTW), ones_f[:],
                     den_acc[:, qt * c.TTW:(qt + 1) * c.TTW],
                     start=True, stop=True, inc=[(sTM, 1)])
            E.wait(v, sTM, sTM.n)
            E.wait(v, sPVs, pv_done_mark[h])
            for qt in range(c.NTT):
                q0 = qt * c.TTW
                E.op(v, "tensor_copy", row_seg("den_r", q0, c.TTW),
                     bank(6 + qt, 0, 1, 0, c.TTW), inc=[(sVE, 1)])
            E.wait(v, sVE, sVE.n)
            rden_r = row("rden_r", c.T)
            E.op(v, "reciprocal", rden_r, den_r, inc=[(sVE, 1)])
            bmark = bounce_rows(
                [(rowbuf[0:1, ROWS["rden_r"][0] + qt * c.TTW:
                         ROWS["rden_r"][0] + (qt + 1) * c.TTW],
                  f32stg[qt][:]) for qt in range(c.NTT)], 6)
            for qt in range(c.NTT):
                q0 = qt * c.TTW
                E.op(v, "tensor_copy", Bsl(h * c.T + q0, c.TTW),
                     bank(4 + qt, 0, 128, 0, c.TTW), inc=[(sVE, 1)])
            oe_marks[h] = sVE.n
            E.wait(v, sVE, oe_marks[h])
            E.wait(v, sBC, bmark)
            for qt in range(c.NTT):
                q0 = qt * c.TTW
                oc = Bsl(h * c.T + q0, c.TTW)
                E.op(v, "tensor_tensor", out=oc, in0=oc,
                     in1=f32stg[qt][:], op=ALU.mult,
                     inc=[(sVE, 1)])
            marks["nrm_%d" % h] = sVE.n
        marks["attn_done"] = sVE.n

    def p5_wo_ln():
        v, s, t, g = "vector", "scalar", "tensor", "gpsimd"
        row_reset()
        ev_marks = {}
        wo_sq = {}
        E.wait("sync", sVE, marks["attn_done"])
        sq_acm = {}

        def emit_wo_stats(gj):
            ttj = gj % 2
            dmj = gj // 2
            t0j = ttj * c.TTW
            aocj = Asl(dmj * c.T + t0j, c.TTW)
            E.wait(t, sVE, ev_marks[gj])
            E.wait(t, sAC, sq_acm[gj])
            E.mm(bank(6 + ttj, 0, 1, 0, c.TTW), ones_bf[:], aocj,
                 start=(gj < 2), stop=(gj >= 2 * (c.DC - 1)))
            E.mm(bank(4 + ttj, 0, 1, 0, c.TTW), ones_bf[:],
                 sqstg[gj % 2][:], start=(gj < 2),
                 stop=(gj >= 2 * (c.DC - 1)), inc=[(sTM, 1)])
            wo_sq[gj] = sTM.n

        for dm in range(c.DC):
            banks = [(2 * dm + tt) % 4 for tt in range(c.NTT)]
            hpw = c.DC // c.WSUB
            do_wd_step()
            for sub in range(c.WSUB):
                src_ap = wob.ap()[dm][:, sub * hpw:(sub + 1) * hpw, :]
                sl, lm = stream_w_subblock(src_ap)
                E.wait(t, sWLs[sl], lm)
                if dm == 0 and sub == 0:
                    E.wait(t, sVE, marks["attn_done"])
                if sub == 0:
                    for tt in range(c.NTT):
                        gi = 2 * dm + tt
                        if gi >= 4:
                            E.wait(t, sVE, ev_marks[gi - 4])
                for hh in range(hpw):
                    ic = sub * hpw + hh
                    last_h = (sub == c.WSUB - 1 and hh == hpw - 1)
                    for tt in range(c.NTT):
                        t0 = tt * c.TTW
                        sub_last = (hh == hpw - 1 and tt == c.NTT - 1)
                        E.mm(bankT(banks[tt]),
                             wsb2[sl][:, hh * 128:(hh + 1) * 128],
                             Bsl(ic * c.T + t0, c.TTW),
                             start=(ic == 0), stop=last_h,
                             inc=[(sWB, 1)] if sub_last else None)
            mm_mark = sWB.n
            for tt in range(c.NTT):
                t0 = tt * c.TTW
                gi = 2 * dm + tt
                if gi >= 2:
                    emit_wo_stats(gi - 2)
                E.wait(v, sWB, mm_mark)
                aoc = Asl(dm * c.T + t0, c.TTW)
                E.op(v, "tensor_scalar", out=aoc, in0=bankT(banks[tt]),
                     scalar1=boc[:, dm:dm + 1], scalar2=None, op0=ALU.add,
                     inc=[(sVE, 1)])
                ev_marks[gi] = sVE.n
                ssl = gi % 2
                E.wait("scalar", sVE, ev_marks[gi])
                if gi >= 2:
                    E.wait("scalar", sTM, wo_sq.get(gi - 2, 0))
                E.op("scalar", "activation", sqstg[ssl][:], aoc, AF.Square,
                     bias=0.0, scale=1.0, inc=[(sAC, 1)])
                sq_acm[gi] = sAC.n
        emit_wo_stats(2 * c.DC - 2)
        emit_wo_stats(2 * c.DC - 1)
        marks["wo_done"] = sWB.n
        E.wait(v, sTM, sTM.n)
        for tt in range(c.NTT):
            t0 = tt * c.TTW
            row_reset()
            sum_r = row("p5_sum", c.TTW)
            sum2_r = row("p5_sum2", c.TTW)
            E.op(v, "tensor_copy", sum_r, bank(6 + tt, 0, 1, 0, c.TTW),
                 inc=[(sVE, 1)])
            E.op(v, "tensor_copy", sum2_r, bank(4 + tt, 0, 1, 0, c.TTW),
                 inc=[(sVE, 1)])
            E.wait(v, sVE, sVE.n)
            r_r, m2_r = row_rstd("p5", sum_r, sum2_r, c.D, c.EPS_LN)
            bc_mark = bounce_rows([(r_r, rbc[0][:]), (m2_r, mbc[0][:])], 0)
            xm_marks = {}
            st2_marks = {}
            Bxm = B.ap().bitcast(F32)
            for ic in range(c.DC):
                sl2 = ic % 2
                if ic >= 2:
                    E.wait(g, sVE, xm_marks[ic - 2])
                E.dma(g, xstg[sl2][:],
                      xT.ap()[ic * 128:(ic + 1) * 128, t0:t0 + c.TTW],
                      inc=[(sXSs[sl2], 16)])
                ld_mark = sXSs[sl2].n
                aoc = Asl(ic * c.T + t0, c.TTW)
                xmc = Bxm[:, ic * c.TTW:(ic + 1) * c.TTW]
                E.wait(v, sXSs[sl2], ld_mark)
                E.wait(v, sBC, bc_mark)
                if ic == 0 and tt == 0:
                    E.wait(v, sWB, marks["wo_done"])
                if ic == 0 and tt > 0:
                    E.wait(v, sVE, marks.get("x2prev", 0))
                if ic >= 2:
                    E.wait(v, sTM, st2_marks[ic - 2])
                E.op(v, "tensor_tensor", out=f32stg[sl2][:], in0=aoc,
                     in1=rbc[0][:], op=ALU.mult, inc=[(sVE, 1)])
                E.wait(v, sVE, sVE.n)
                E.op(v, "tensor_tensor", out=f32stg[sl2][:],
                     in0=f32stg[sl2][:], in1=mbc[0][:], op=ALU.add,
                     inc=[(sVE, 1)])
                E.wait(v, sVE, sVE.n)
                E.op(v, "tensor_tensor", out=xmc,
                     in0=f32stg[sl2][:], in1=xstg[sl2][:], op=ALU.add,
                     inc=[(sVE, 1)])
                xm_marks[ic] = sVE.n
                E.wait(g, sVE, xm_marks[ic])
                E.dma(g, xmid_d.ap()[ic * 128:(ic + 1) * 128, t0:t0 + c.TTW],
                      xmc, inc=[(sGD, 16)])
                E.wait(v, sVE, xm_marks[ic])
                E.op(v, "tensor_tensor", out=f32stg[sl2][:],
                     in0=xmc, in1=xmc, op=ALU.mult, inc=[(sVE, 1)])
                E.wait(t, sVE, sVE.n)
                E.mm(bank(6 + tt, 0, 1, 0, c.TTW), ones_f[:], xmc,
                     start=(ic == 0), stop=(ic == c.DC - 1))
                E.mm(bank(4 + tt, 0, 1, 0, c.TTW), ones_f[:], f32stg[sl2][:],
                     start=(ic == 0), stop=(ic == c.DC - 1), inc=[(sTM, 1)])
                st2_marks[ic] = sTM.n
            marks[f"xm_done_{tt}"] = sGD.n
            E.wait(v, sTM, sTM.n)
            row_reset()
            sumb_r = row("p5b_sum", c.TTW)
            sumb2_r = row("p5b_sum2", c.TTW)
            E.op(v, "tensor_copy", sumb_r, bank(6 + tt, 0, 1, 0, c.TTW),
                 inc=[(sVE, 1)])
            E.op(v, "tensor_copy", sumb2_r, bank(4 + tt, 0, 1, 0, c.TTW),
                 inc=[(sVE, 1)])
            E.wait(v, sVE, sVE.n)
            r2_r, m22_r = row_rstd("p5b", sumb_r, sumb2_r, c.D, c.EPS_LN)
            bc2_mark = bounce_rows([(r2_r, rbc[1][:]), (m22_r, mbc[1][:])], 3)
            x2_marks = {}
            x2st_marks = {}
            Bxm = B.ap().bitcast(F32)
            for ic in range(c.DC):
                sl2 = ic % 2
                xmc = Bxm[:, ic * c.TTW:(ic + 1) * c.TTW]
                E.wait(v, sBC, bc2_mark)
                E.op(v, "tensor_tensor", out=f32stg[sl2][:], in0=xmc,
                     in1=rbc[1][:], op=ALU.mult, inc=[(sVE, 1)])
                E.wait(v, sVE, sVE.n)
                if ic >= 1:
                    E.wait(v, sGD, x2st_marks[ic - 1])
                E.op(v, "tensor_tensor", out=bfstg[sl2][:],
                     in0=f32stg[sl2][:], in1=mbc[1][:], op=ALU.add,
                     inc=[(sVE, 1)])
                x2_marks[ic] = sVE.n
                E.wait(g, sVE, x2_marks[ic])
                E.dma(g, x2_d.ap()[ic * 128:(ic + 1) * 128, t0:t0 + c.TTW],
                      bfstg[sl2][:], inc=[(sGD, 16)])
                x2st_marks[ic] = sGD.n
            marks["x2prev"] = sVE.n
        marks["x2_done"] = sGD.n

    def p5b_scales_and_gathers():
        v, t, g = "vector", "tensor", "gpsimd"
        row_reset()
        while wd_state["i"] < len(WD_STEPS):
            do_wd_step()
        E.wait(g, sGD, marks["x2_done"])
        Bf = B.ap().bitcast(F32)
        for src, dst, rows_, cols_ in ((sg, sgbf_d, c.G, c.FI),
                                       (su, subf_d, c.G, c.FI)):
            n = rows_ * cols_
            w = n // 128
            E.wait(g, sVE, sVE.n)
            E.dma(g, Bf[:, 0:w],
                  src.ap().rearrange("a b -> (a b)").rearrange(
                      "(p w) -> p w", p=128), inc=[(sGD, 16)])
            E.wait(v, sGD, sGD.n)
            E.op(v, "tensor_copy", B[:, 2 * SCW:2 * SCW + w], Bf[:, 0:w],
                 inc=[(sVE, 1)])
            E.wait(g, sVE, sVE.n)
            E.dma(g, dst.ap().rearrange("a b -> (a b)").rearrange(
                "(p w) -> p w", p=128), B[:, 2 * SCW:2 * SCW + w],
                inc=[(sGD, 16)])
            E.wait(g, sGD, sGD.n)
        marks["scbf"] = sGD.n
        for jj in range(c.FJC):
            for hf in range(2):
                src = AP(tensor=x2_d.ap().tensor,
                         offset=(2 * jj + hf) * 128 * c.T,
                         ap=[[2 * c.T, 64], [1, c.T]])
                E.dma(g, Asl(A_xe + jj * c.T, c.T, 64 * hf, 64 * hf + 64),
                      src, inc=[(sGD, 16)])
                src = AP(tensor=x2_d.ap().tensor,
                         offset=(2 * jj + hf) * 128 * c.T + c.T,
                         ap=[[2 * c.T, 64], [1, c.T]])
                E.dma(g, Asl(A_xo + jj * c.T, c.T, 64 * hf, 64 * hf + 64),
                      src, inc=[(sGD, 16)])
        E.wait(v, sGD, sGD.n)
        E.op(v, "tensor_scalar", out=Asl(A_xe, c.FJC * c.T),
             in0=Asl(A_xe, c.FJC * c.T), scalar1=1.0 / 16.0, scalar2=None,
             op0=ALU.mult, inc=[(sVE, 1)])
        marks["xe_ready"] = sVE.n
        xsk = {}
        for tt in range(c.NTT):
            t0 = tt * c.TTW
            for icd in range(c.DC):
                gidx = tt * c.DC + icd
                sl4 = gidx % 4
                if gidx >= 4:
                    E.wait(g, sTM, xsk[gidx - 4])
                E.dma(g, bfstg[sl4][:],
                      x2_d.ap()[icd * 128:(icd + 1) * 128, t0:t0 + c.TTW],
                      inc=[(sBFs[sl4], 16)])
                E.wait(t, sBFs[sl4], sBFs[sl4].n)
                if icd == 0 and tt > 0:
                    E.wait(t, sVE, marks["xsum_cp"])
                m = icd % 64
                E.mm(bank(0, 0, min(c.G, 128), 0, c.TTW),
                     Gpat[:, 128 - 2 * m:128 - 2 * m + min(c.G, 128)],
                     bfstg[sl4][:], start=(icd == 0),
                     stop=(icd == c.DC - 1), inc=[(sTM, 1)])
                xsk[gidx] = sTM.n
            E.wait(v, sTM, sTM.n)
            E.op(v, "tensor_copy", xsum_x[:, t0:t0 + c.TTW],
                 bank(0, 0, min(c.G, 128), 0, c.TTW), inc=[(sVE, 1)])
            marks["xsum_cp"] = sVE.n
        marks["xsum_x"] = sVE.n

    def p6_gate_up():
        v, s, t, g = "vector", "scalar", "tensor", "gpsimd"
        row_reset()
        E.wait("sync", sWD, sWD.n)
        E.wait(v, sWD, sWD.n)
        E.wait(g, sWD, sWD.n)
        xh0 = Af32(A_xh_f32, c.T, 0, c.DG_T0)
        xh1 = Af32(A_xh_f32 + c.T, c.T, 0, max(c.DG_T1, 1))
        E.wait(v, sVE, marks["xe_ready"])
        E.op(v, "memset", xh0, 0.0, inc=[(sVE, 1)])
        if c.DG_T1 > 0:
            E.op(v, "memset", xh1, 0.0, inc=[(sVE, 1)])
        E.wait(t, sVE, marks["xe_ready"])
        E.wait(t, sVE, marks["xsum_x"])
        E.wait(s, sVE, marks["xe_ready"])
        E.wait("sync", sVE, marks["xsum_x"])
        dqm = {}
        dqh = {}
        dqmp = {}
        mmdone = {}
        hdone = {}
        hst_marks = {}
        xh_marks = {}
        xhmm_marks = {}
        for ic in range(c.FIC):
            sl = ic % 2
            pkg = Bi32(b_pk_g // 2, FJW)
            pku = Bi32(b_pk_u // 2, FJW)
            if ic % 2 == 0:
                sxg = Bsl(b_sx_g, FJW)
                sxu = Bsl(b_sx_u, FJW)
            else:
                sxg = Asl(A_sx2, FJW)
                sxu = Asl(A_sx2 + FJW, FJW)
            hig = Bsl(b_hi_g[sl], FJW)
            log_ = Bsl(b_lo_g[sl], FJW)
            hiu = Bsl(b_hi_u[sl], FJW)
            lou = Bsl(b_lo_u[sl], FJW)
            HWF = FJW // 2
            FJH = c.FJC // 2
            pkh_mark = {}
            for h in range(2):
                if ic >= 1:
                    E.wait("sync", sVE, dqh[(ic - 1, h)])
                E.dma("sync",
                      Bi32(b_pk_g // 2 + h * HWF, HWF).rearrange(
                          "p (jc i) -> p jc i", i=128),
                      wgp.ap()[ic][:, h * FJH:(h + 1) * FJH, :],
                      inc=[(sPKH[h], 16)])
                E.dma("sync",
                      Bi32(b_pk_u // 2 + h * HWF, HWF).rearrange(
                          "p (jc i) -> p jc i", i=128),
                      wup.ap()[ic][:, h * FJH:(h + 1) * FJH, :],
                      inc=[(sPKH[h], 16)])
                pkh_mark[h] = sPKH[h].n
            if ic == 0:
                E.wait("sync", sGD, marks["scbf"])
            if ic >= 2:
                E.wait("sync", sVE, dqm[ic - 2])
            for b in range(4):
                src = AP(tensor=sgbf_d.ap().tensor,
                         offset=b * c.FI + ic * 128,
                         ap=[[0, 32], [4 * c.FI, c.FJC], [1, 128]])
                E.dma("sync", sxg[32 * b:32 * b + 32, :].rearrange(
                    "p (jc i) -> p jc i", i=128), src,
                    inc=[(sSXL[ic % 2], 16)])
                src = AP(tensor=subf_d.ap().tensor,
                         offset=b * c.FI + ic * 128,
                         ap=[[0, 32], [4 * c.FI, c.FJC], [1, 128]])
                E.dma("sync", sxu[32 * b:32 * b + 32, :].rearrange(
                    "p (jc i) -> p jc i", i=128), src,
                    inc=[(sSXL[ic % 2], 16)])
            if ic >= 1:
                E.wait("sync", sVE, dqm[ic - 1])
            E.dma("sync", szs[0],
                  sg.ap()[:, ic * 128:(ic + 1) * 128], inc=[(sSY, 16)])
            E.dma("sync", szs[1],
                  zg.ap()[:, ic * 128:(ic + 1) * 128], inc=[(sSY, 16)])
            E.dma("sync", szs[2],
                  su.ap()[:, ic * 128:(ic + 1) * 128], inc=[(sSY, 16)])
            E.dma("sync", szs[3],
                  zu.ap()[:, ic * 128:(ic + 1) * 128], inc=[(sSY, 16)])
            ld_mark = sSY.n
            tmp6 = Bi32(b_tmp6 // 2, FJW // 2)
            tmp6b = Bi32(b_tmp6b // 2, FJW // 2)
            for h in range(2):
                E.wait(v, sPKH[h], pkh_mark[h])
                if h == 0:
                    E.wait(v, sSXL[ic % 2], sSXL[ic % 2].n)
                if h == 0 and ic >= 2:
                    E.wait(v, sTM, mmdone[ic - 2])
                a = h * HWF
                sxg_h = (Bsl(b_sx_g + a, HWF) if ic % 2 == 0
                         else Asl(A_sx2 + a, HWF))
                sxu_h = (Bsl(b_sx_u + a, HWF) if ic % 2 == 0
                         else Asl(A_sx2 + FJW + a, HWF))
                emit_nibbles(Bi32(b_pk_g // 2 + a, HWF),
                             Bsl(b_hi_g[sl] + a, HWF),
                             Bsl(b_lo_g[sl] + a, HWF),
                             tmp6, HWF, sx_bf=sxg_h)
                emit_nibbles(Bi32(b_pk_u // 2 + a, HWF),
                             Bsl(b_hi_u[sl] + a, HWF),
                             Bsl(b_lo_u[sl] + a, HWF),
                             tmp6b, HWF, sx_bf=sxu_h)
                dqh[(ic, h)] = sVE.n
            E.wait(v, sSY, ld_mark)
            E.op(v, "scalar_tensor_tensor", out=szn_g[sl],
                 in0=szs[0], scalar=-1.0, in1=szs[1],
                 op0=ALU.mult, op1=ALU.mult, inc=[(sVE, 1)])
            E.op(v, "scalar_tensor_tensor", out=szn_u[sl],
                 in0=szs[2], scalar=-1.0, in1=szs[3],
                 op0=ALU.mult, op1=ALU.mult, inc=[(sVE, 1)])
            dqm[ic] = sVE.n
            dqmp[ic] = sPO.n
            def emit_xh(icp):
                for ttp in range(c.NTT):
                    t0p = ttp * c.TTW
                    bkgp = (icp % 2) * 4 + 2 * ttp
                    E.wait(t, sVE, hdone[(icp, ttp)])
                    m = icp % 64
                    tile1 = (2 * icp) >= 128
                    gw = max(c.DG_T1 if tile1 else c.DG_T0, 1)
                    E.mm(bank(bkgp, 0, gw, 0, c.TTW),
                         Gpat[:, 128 - 2 * m:128 - 2 * m + gw],
                         Asl(b_hstg[icp % 2] + t0p, c.TTW),
                         start=True, stop=True, inc=[(sTM, 1)])
                    xhmm_marks[(icp, ttp)] = sTM.n
                    E.wait(v, sTM, sTM.n)
                    xsb = xh1 if tile1 else xh0
                    E.op(v, "tensor_tensor",
                         out=xsb[:, t0p:t0p + c.TTW],
                         in0=xsb[:, t0p:t0p + c.TTW],
                         in1=bank(bkgp, 0, gw, 0, c.TTW), op=ALU.add,
                         inc=[(sVE, 1)])
                    xh_marks[(icp, ttp)] = sVE.n

            if ic >= 1:
                emit_xh(ic - 1)
            E.wait(t, sVE, dqm[ic])
            E.wait(t, sPO, dqmp[ic])
            for tt in range(c.NTT):
                t0 = tt * c.TTW
                bkg = (ic % 2) * 4 + 2 * tt
                bku = bkg + 1
                if ic >= 2:
                    E.wait(t, sVE, hdone[(ic - 2, tt)])
                    E.wait(t, sVE, xh_marks[(ic - 2, tt)])
                for jc in range(c.FJC):
                    E.mm(bankT(bkg), hig[:, jc * 128:(jc + 1) * 128],
                         Asl(A_xe + jc * c.T + t0, c.TTW),
                         start=(jc == 0), stop=False)
                    E.mm(bankT(bkg), log_[:, jc * 128:(jc + 1) * 128],
                         Asl(A_xo + jc * c.T + t0, c.TTW),
                         start=False, stop=False)
                    E.mm(bankT(bku), hiu[:, jc * 128:(jc + 1) * 128],
                         Asl(A_xe + jc * c.T + t0, c.TTW),
                         start=(jc == 0), stop=False)
                    E.mm(bankT(bku), lou[:, jc * 128:(jc + 1) * 128],
                         Asl(A_xo + jc * c.T + t0, c.TTW),
                         start=False, stop=False)
                E.mm(bankT(bkg), szn_g[sl],
                     xsum_x[0:c.G, t0:t0 + c.TTW], start=False, stop=True)
                E.mm(bankT(bku), szn_u[sl],
                     xsum_x[0:c.G, t0:t0 + c.TTW], start=False, stop=True,
                     inc=[(sTM, 1)])
                mmdone[ic] = sTM.n
                gsb = bfstg[(2 * ic + tt) % 4][:]
                E.wait(s, sTM, mmdone[ic])
                prev = hdone.get((ic - 2, tt))
                if prev is not None:
                    E.wait(s, sVE, prev)
                E.op(s, "activation", gsb, bankT(bkg), AF.Silu, bias=0.0,
                     scale=1.0, inc=[(sAC, 1)])
                E.wait(v, sAC, sAC.n)
                if ic >= 2 and tt == 0:
                    E.wait(v, sHSs[ic % 2], hst_marks[ic - 2])
                if (ic - 2, tt) in xhmm_marks:
                    E.wait(v, sTM, xhmm_marks[(ic - 2, tt)])
                E.op(v, "tensor_tensor",
                     out=Asl(b_hstg[sl] + t0, c.TTW), in0=gsb, in1=bankT(bku),
                     op=ALU.mult, inc=[(sVE, 1)])
                hdone[(ic, tt)] = sVE.n
            E.wait(g, sVE, sVE.n)
            E.dma(g, h_d.ap()[ic * 128:(ic + 1) * 128, :],
                  Asl(b_hstg[sl], c.T), inc=[(sHSs[sl], 16)])
            hst_marks[ic] = sHSs[sl].n
        emit_xh(c.FIC - 1)
        marks["h_done_a"] = sHSs[0].n
        marks["h_done_b"] = sHSs[1].n
        E.wait(v, sVE, sVE.n)
        E.op(v, "tensor_copy", xh_bf0, xh0, inc=[(sVE, 1)])
        if c.DG_T1 > 0:
            E.op(v, "tensor_copy", xh_bf1, xh1, inc=[(sVE, 1)])
        marks["xh_bf"] = sVE.n
        marks["p6_done"] = sVE.n
        marks["p6_hs0"] = sHSs[0].n
        marks["p6_hs1"] = sHSs[1].n

    def p7_down_and_out():
        v, s, t, g = "vector", "scalar", "tensor", "gpsimd"
        WSL = c.FIC * 128
        assert 2 * WSL <= B_N
        ffn_done = {}
        for hf in range(c.NTT):
            t0 = hf * c.TTW
            row_reset()
            E.wait(g, sGD, sGD.n)
            E.wait(g, sWD, sWD.n)
            E.wait(g, sHSs[0], marks["h_done_a"])
            E.wait(g, sHSs[1], marks["h_done_b"])
            E.wait(g, sVE, marks["p6_done"])
            E.wait(g, sTM, sTM.n)
            E.wait(g, sVE, marks.get("p7_free", 0))
            for ic in range(c.FIC):
                E.dma(g, Asl(ic * c.TTW, c.TTW),
                      h_d.ap()[ic * 128:(ic + 1) * 128, t0:t0 + c.TTW],
                      inc=[(sGD, 16)])
            he_mark = sGD.n
            evs = {}
            wld = {}
            ffnst_marks = {}
            p7st_marks = {}
            for hc in range(c.DC):
                sl = hc % 2
                if hc == 0 and hf == 0:
                    E.wait("sync", sVE, marks["p6_done"])
                    E.wait("sync", sTM, sTM.n)
                if hc >= 2:
                    E.wait("sync", sTM, evs[hc - 2])
                elif hf > 0:
                    E.wait("sync", sTM,
                           marks["p7mm_hm2"] if hc == 0 else marks["p7mm"])
                wsrc = AP(tensor=wdeq_d.ap().tensor,
                          offset=hc * c.FI * 128,
                          ap=[[128, 128], [128 * 128, c.FIC], [1, 128]])
                E.dma("sync",
                      Bsl(sl * WSL, WSL).rearrange(
                          "p (ic hh) -> p ic hh", hh=128),
                      wsrc, inc=[(sWLs[sl], 16)])
                wld[hc] = sWLs[sl].n
                E.wait("sync", sWLs[sl], sWLs[sl].n)
                if hc >= 1 or hf > 0:
                    E.wait("sync", sVE, marks.get("p7szn", 0))
                E.dma("sync", szsd[0],
                      sd.ap()[0:c.DG_T0, hc * 128:(hc + 1) * 128],
                      inc=[(sSY, 16)])
                E.dma("sync", szsd[1],
                      zd.ap()[0:c.DG_T0, hc * 128:(hc + 1) * 128],
                      inc=[(sSY, 16)])
                if c.DG_T1 > 0:
                    E.dma("sync", szsd[2][0:c.DG_T1, :],
                          sd.ap()[128:128 + c.DG_T1,
                                  hc * 128:(hc + 1) * 128],
                          inc=[(sSY, 16)])
                    E.dma("sync", szsd[3][0:c.DG_T1, :],
                          zd.ap()[128:128 + c.DG_T1,
                                  hc * 128:(hc + 1) * 128],
                          inc=[(sSY, 16)])
                szld = sSY.n
                E.wait(v, sSY, szld)
                if hc >= 2:
                    E.wait(v, sTM, evs[hc - 2])
                elif hf > 0:
                    E.wait(v, sTM,
                           marks["p7mm_hm2"] if hc == 0 else marks["p7mm"])
                E.op(v, "scalar_tensor_tensor", out=sznd0[sl],
                     in0=szsd[0], scalar=-1.0, in1=szsd[1],
                     op0=ALU.mult, op1=ALU.mult, inc=[(sVE, 1)])
                if c.DG_T1 > 0:
                    E.op(v, "scalar_tensor_tensor", out=sznd1[sl],
                         in0=szsd[2][0:c.DG_T1, :], scalar=-1.0,
                         in1=szsd[3][0:c.DG_T1, :], op0=ALU.mult,
                         op1=ALU.mult, inc=[(sVE, 1)])
                szn_mark = sVE.n
                marks["p7szn"] = szn_mark
                bk_i = hc % 2
                E.wait(t, sGD, he_mark)
                E.wait(t, sVE, szn_mark)
                E.wait(t, sVE, marks["xh_bf"])
                E.wait(t, sWLs[hc % 2], wld[hc])
                if hc >= 2:
                    E.wait(t, sVE, evs.get((hc - 2, "ev"), 0))
                elif hf > 0:
                    E.wait(t, sVE, marks.get("p7evv", 0))
                for ic in range(c.FIC):
                    E.mm(bankT(bk_i),
                         Bsl(sl * WSL + ic * 128, 128),
                         Asl(ic * c.TTW, c.TTW),
                         start=(ic == 0), stop=False)
                E.mm(bankT(bk_i), sznd0[sl],
                     xh_bf0[0:c.DG_T0, t0:t0 + c.TTW],
                     start=False, stop=(c.DG_T1 == 0),
                     inc=[(sTM, 1)] if c.DG_T1 == 0 else None)
                if c.DG_T1 > 0:
                    E.mm(bankT(bk_i), sznd1[sl][0:c.DG_T1, :],
                         xh_bf1[0:c.DG_T1, t0:t0 + c.TTW],
                         start=False, stop=True, inc=[(sTM, 1)])
                evs[hc] = sTM.n
                marks["p7mm"] = sTM.n
                if hc == c.DC - 2:
                    marks["p7mm_hm2"] = sTM.n
                E.wait(v, sTM, evs[hc])
                if hc >= 1:
                    E.wait(v, sGD, ffnst_marks[hc - 1])
                elif hf > 0:
                    E.wait(v, sGD, marks.get("p7ffnst", 0))
                fo = bfstg[2 + hc % 2][:]
                E.op(v, "tensor_copy", fo, bankT(bk_i), inc=[(sVE, 1)])
                evs[(hc, "ev")] = sVE.n
                E.wait(g, sVE, evs[(hc, "ev")])
                E.dma(g, ffn_d.ap()[hc * 128:(hc + 1) * 128, t0:t0 + c.TTW],
                      fo, inc=[(sGD, 16)])
                ffnst_marks[hc] = sGD.n
                ssq = sqstg[hc % 2][:]
                E.wait(v, sVE, evs[(hc, "ev")])
                if hc >= 2:
                    E.wait(v, sTM, p7st_marks[hc - 2])
                E.op(v, "tensor_tensor", out=ssq, in0=fo, in1=fo,
                     op=ALU.mult, inc=[(sVE, 1)])
                E.wait(t, sVE, sVE.n)
                E.mm(bank(6 if hf == 0 else 7, 0, 1, 0, c.TTW),
                     ones_bf[:], fo, start=(hc == 0), stop=(hc == c.DC - 1))
                E.mm(bank(4 if hf == 0 else 5, 0, 1, 0, c.TTW),
                     ones_bf[:], ssq, start=(hc == 0), stop=(hc == c.DC - 1),
                     inc=[(sTM, 1)])
                p7st_marks[hc] = sTM.n
            marks["p7evv"] = sVE.n
            marks["p7ffnst"] = sGD.n
            ffn_done[hf] = sGD.n
            marks["p7_free"] = sVE.n
        for tt in range(c.NTT):
            t0 = tt * c.TTW
            row_reset()
            E.wait(v, sTM, sTM.n)
            sum_r = row("p7_sum", c.TTW)
            sum2_r = row("p7_sum2", c.TTW)
            E.op(v, "tensor_copy", sum_r,
                 bank(6 if tt == 0 else 7, 0, 1, 0, c.TTW), inc=[(sVE, 1)])
            E.op(v, "tensor_copy", sum2_r,
                 bank(4 if tt == 0 else 5, 0, 1, 0, c.TTW), inc=[(sVE, 1)])
            E.wait(v, sVE, sVE.n)
            r_r, m2_r = row_rstd("p7", sum_r, sum2_r, c.D, c.EPS_LN)
            bc_mark = bounce_rows([(r_r, rbc[0][:]), (m2_r, mbc[0][:])], 0)
            E.wait(g, sGD, marks["p7ffnst"])
            out_marks = {}
            ost_marks = {}
            ost_base = sOUT.n
            for ic in range(c.DC):
                sl4 = ic % 4
                sl2 = ic % 2
                if ic >= 2:
                    E.wait(g, sVE, out_marks[ic - 2])
                elif tt > 0:
                    E.wait(g, sVE, marks.get("p7out_prev", 0))
                E.wait(g, sBC, bc_mark)
                E.dma(g, bfstg[sl4][:],
                      ffn_d.ap()[ic * 128:(ic + 1) * 128, t0:t0 + c.TTW],
                      inc=[(sBFs[sl4], 16)])
                E.dma(g, xstg[sl2][:],
                      xmid_d.ap()[ic * 128:(ic + 1) * 128, t0:t0 + c.TTW],
                      inc=[(sXSs[sl2], 16)])
                E.wait(v, sBFs[sl4], sBFs[sl4].n)
                E.wait(v, sXSs[sl2], sXSs[sl2].n)
                E.op(v, "tensor_tensor", out=f32stg[sl2][:],
                     in0=bfstg[sl4][:], in1=rbc[0][:], op=ALU.mult,
                     inc=[(sVE, 1)])
                E.wait(v, sVE, sVE.n)
                E.op(v, "tensor_tensor", out=f32stg[sl2][:],
                     in0=f32stg[sl2][:], in1=mbc[0][:], op=ALU.add,
                     inc=[(sVE, 1)])
                E.wait(v, sVE, sVE.n)
                E.wait(v, sOUT, ost_marks.get(ic - 1, ost_base))
                E.op(v, "tensor_tensor", out=f32stg[2 + sl2][:],
                     in0=f32stg[sl2][:], in1=xstg[sl2][:], op=ALU.add,
                     inc=[(sVE, 1)])
                out_marks[ic] = sVE.n
                E.wait(g, sVE, out_marks[ic])
                E.dma(g, outT.ap()[ic * 128:(ic + 1) * 128, t0:t0 + c.TTW],
                      f32stg[2 + sl2][:], inc=[(sOUT, 16)])
                ost_marks[ic] = sOUT.n
            marks["p7out_prev"] = sVE.n

    def epilogue():
        E.wait("gpsimd", sOUT, sOUT.n)

    def program():
        marks.clear()
        row_reset()
        wblk[0] = 0
        wd_state["i"] = 0
        wd_state["nib"] = {}
        wd_state["ld"] = {}
        wd_state["st"] = {}
        p0_state["j"] = 0
        p0_state["pend"] = []
        p0_state["final"] = None
        prologue()
        p1_ln1()
        p0_drain()
        p2_qkv()
        p3_collect()
        p4_attention()
        p5_wo_ln()
        p5b_scales_and_gathers()
        p6_gate_up()
        p7_down_and_out()
        epilogue()

    with nc.Block() as block:
        for eng in Emitter.ENGS:
            def mk(e):
                def f(_engine):
                    E.role = e
                    E.reset()
                    program()
                return f
            getattr(block, eng)(mk(eng))

    ctx.close()
    return nc, c


def prep_inputs(c, kw):
    f = np.asarray
    x = f(kw["L0_attn_hidden_states"], dtype=np.float32).reshape(c.S, c.D)
    xTf = np.ascontiguousarray(x.T)

    def blk_imaj(W):
        W = f(W, dtype=np.float32)
        return np.ascontiguousarray(
            W.reshape(c.DC, 128, c.DC, 128).transpose(2, 1, 0, 3))

    def colvec(b):
        return np.ascontiguousarray(
            f(b, dtype=np.float32).reshape(c.DC, 128).T)

    def pack_gu(Wq):
        Wt = f(Wq, dtype=np.int32).T
        return np.ascontiguousarray(
            Wt.reshape(c.FJC, 128, c.FIC, 128).transpose(2, 1, 0, 3))

    def pack_d(Wq):
        Wt = f(Wq, dtype=np.int32).T
        return np.ascontiguousarray(
            Wt.reshape(c.DJC, 128, c.DC, 128).transpose(2, 1, 0, 3))

    def tr(a):
        return np.ascontiguousarray(f(a, dtype=np.float32).T)

    shared = {
        "wq": blk_imaj(kw["L0_attn_Wq"]),
        "wk": blk_imaj(kw["L0_attn_Wk"]),
        "wv": blk_imaj(kw["L0_attn_Wv"]),
        "wo": blk_imaj(kw["L0_attn_Wo"]),
        "bq": colvec(kw["L0_attn_bq"]),
        "bk": colvec(kw["L0_attn_bk"]),
        "bv": colvec(kw["L0_attn_bv"]),
        "bo": colvec(kw["L0_attn_bo"]),
        "gq": colvec(kw["L0_attn_gq"]),
        "gk": colvec(kw["L0_attn_gk"]),
        "wgp": pack_gu(kw["L0_ffn_gate_Wq"]),
        "wup": pack_gu(kw["L0_ffn_up_Wq"]),
        "wdp": pack_d(kw["L0_ffn_down_Wq"]),
        "sg": tr(kw["L0_ffn_gate_scale"]),
        "zg": tr(kw["L0_ffn_gate_zero"]),
        "su": tr(kw["L0_ffn_up_scale"]),
        "zu": tr(kw["L0_ffn_up_zero"]),
        "sd": tr(kw["L0_ffn_down_scale"]),
        "zd": tr(kw["L0_ffn_down_zero"]),
    }
    in_maps = []
    for core in range(NCORES):
        m = dict(shared)
        m["xT"] = np.ascontiguousarray(xTf[:, core * c.T:(core + 1) * c.T])
        in_maps.append(m)
    return in_maps


def kernel(**kw):
    c = Cfg(full_cfg())
    nc, _ = build_program(full_cfg())
    in_maps = prep_inputs(c, kw)
    res = run_bass_kernel_spmd(nc, in_maps, core_ids=list(range(NCORES)))
    outs = [np.asarray(r["outT"]) for r in res.results]
    full = np.concatenate(outs, axis=1)
    return np.ascontiguousarray(full.T).reshape(1, c.S, c.D).astype(np.float32)


# version 22 (final)
# speedup vs baseline: 1.0410x; 1.0410x over previous
import math
import contextlib
import numpy as np
import concourse.bass as bass
import concourse.mybir as mybir
from concourse.bass_types import AP
from concourse.bass_utils import run_bass_kernel_spmd

F32 = mybir.dt.float32
BF16 = mybir.dt.bfloat16
I32 = mybir.dt.int32
AF = mybir.ActivationFunctionType
ALU = mybir.AluOpType
AXX = mybir.AxisListType.X

NCORES = 8


def full_cfg():
    return dict(D=4096, T=1024, FI=11008, TTW=512)


def small_cfg():
    return dict(D=512, T=256, FI=1280, TTW=128)


class Cfg:
    def __init__(self, d):
        self.D = d["D"]
        self.T = d["T"]
        self.FI = d["FI"]
        self.TTW = d["TTW"]
        self.DC = self.D // 128
        self.S = NCORES * self.T
        self.H = self.D // 128
        self.HD = 128
        self.NTT = self.T // self.TTW
        self.KC = self.S // 128
        self.FIC = self.FI // 128
        self.FJ = self.D // 2
        self.FJC = self.FJ // 128
        self.G = self.D // 64
        self.FJD = self.FI // 2
        self.DJC = self.FJD // 128
        self.DG = self.FI // 64
        self.DJH = (self.DJC + 1) // 2
        self.EPS_LN = 1e-5
        self.EPS_RMS = 1e-6
        self.DG_T0 = min(self.DG, 128)
        self.DG_T1 = self.DG - self.DG_T0
        self.WSUB = 4
        assert self.DC % self.WSUB == 0
        assert self.D % 128 == 0 and self.FI % 256 == 0
        assert self.T % self.TTW == 0 and self.TTW <= 512
        assert self.TTW % 128 == 0
        assert self.G <= 64 or self.G == 64, "gate/up groups must fit 64"


class Sem:
    __slots__ = ("h", "n")

    def __init__(self, h):
        self.h = h
        self.n = 0


class Emitter:
    ENGS = ("sync", "tensor", "scalar", "vector", "gpsimd")

    def __init__(self, nc):
        self.nc = nc
        self.role = None
        self.sems = []

    def sem(self, name):
        s = Sem(self.nc.alloc_semaphore(name=name))
        self.sems.append(s)
        return s

    def reset(self):
        for s in self.sems:
            s.n = 0

    def op(self, eng, fname, *args, inc=None, **kw):
        inst = None
        if self.role == eng:
            inst = getattr(getattr(self.nc, eng), fname)(*args, **kw)
            if inc:
                for s, v in inc:
                    inst.then_inc(s.h, v)
        if inc:
            for s, v in inc:
                s.n += v
        return inst

    def wait(self, eng, s, val):
        if self.role == eng:
            getattr(self.nc, eng).wait_ge(s.h, val)

    def mm(self, out, lhsT, rhs, start, stop, inc=None):
        self.op("tensor", "matmul", out, lhsT, rhs, start=start, stop=stop,
                inc=inc)

    def dma(self, eng, out, in_, inc=None, **kw):
        self.op(eng, "dma_start", out=out, in_=in_, inc=inc, **kw)


def build_program(cfg_dict):
    c = Cfg(cfg_dict)
    nc = bass.Bass("TRN2", target_bir_lowering=False, debug=False)
    E = Emitter(nc)
    RSQ = 1.0 / math.sqrt(c.HD)
    CS_OFF = 25.0

    def din(name, shape, dt):
        return nc.dram_tensor(name, shape, dt, kind="ExternalInput")

    def dint(name, shape, dt):
        return nc.dram_tensor(name, shape, dt, kind="Internal")

    xT = din("xT", [c.D, c.T], F32)
    wq = din("wq", [c.DC, 128, c.DC, 128], F32)
    wk = din("wk", [c.DC, 128, c.DC, 128], F32)
    wv = din("wv", [c.DC, 128, c.DC, 128], F32)
    wo = din("wo", [c.DC, 128, c.DC, 128], F32)
    bq = din("bq", [128, c.DC], F32)
    bk = din("bk", [128, c.DC], F32)
    bv = din("bv", [128, c.DC], F32)
    bo = din("bo", [128, c.DC], F32)
    gq = din("gq", [128, c.DC], F32)
    gk = din("gk", [128, c.DC], F32)
    wgp = din("wgp", [c.FIC, 128, c.FJC, 128], I32)
    wup = din("wup", [c.FIC, 128, c.FJC, 128], I32)
    wdp = din("wdp", [c.DC, 128, c.DJC, 128], I32)
    sg = din("sg", [c.G, c.FI], F32)
    zg = din("zg", [c.G, c.FI], F32)
    su = din("su", [c.G, c.FI], F32)
    zu = din("zu", [c.G, c.FI], F32)
    sd = din("sd", [c.DG, c.D], F32)
    zd = din("zd", [c.DG, c.D], F32)
    outT = nc.dram_tensor("outT", [c.D, c.T], F32, kind="ExternalOutput")

    kc_d = dint("kc_d", [c.D + 1, c.T], BF16)
    vc_d = dint("vc_d", [c.H * c.T, c.HD], BF16)
    kall = dint("kall", [NCORES * (c.D + 1), c.T], BF16)
    vall = dint("vall", [NCORES * c.H * c.T, c.HD], BF16)
    rbounce = dint("rbounce", [8, max(c.T, 128)], F32)
    xmid_d = dint("xmid_d", [c.D, c.T], F32)
    x2_d = dint("x2_d", [c.D, c.T], BF16)
    h_d = dint("h_d", [c.FI, c.T], BF16)
    ffn_d = dint("ffn_d", [c.D, c.T], BF16)
    sgbf_d = dint("sgbf_d", [c.G, c.FI], BF16)
    subf_d = dint("subf_d", [c.G, c.FI], BF16)
    wdeq_d = dint("wdeq_d", [c.DC, c.FI, 128], BF16)

    ctx = contextlib.ExitStack()

    def sbuf(name, shape, dt):
        return ctx.enter_context(nc.sbuf_tensor(name, shape, dt))

    NPT = 4
    A_N = max(c.DC * c.T,
              4 * c.KC * 128 + NPT * c.T + 2 * c.T,
              2 * c.FJC * c.T + 6 * c.T + 2 * c.FJC * 128,
              2 * c.DJC * c.TTW)
    B_N = max(c.DC * c.T, 2 * c.DC * c.TTW,
              16 * c.FJC * 128,
              11 * c.DJH * 128,
              3 * (max(c.G * c.FI, c.DG * c.D) // 128))
    A = sbuf("arenaA", [128, A_N], BF16)
    B = sbuf("arenaB", [128, B_N], BF16)

    WBW = (c.DC // c.WSUB) * 128
    wst = [sbuf(f"wst{i}", [128, WBW], F32) for i in range(2)]
    wsb = [sbuf(f"wsb{i}", [128, WBW], BF16) for i in range(2)]

    rowbuf = sbuf("rowbuf", [128, 4160], F32)
    rbc = [sbuf(f"rbc{i}", [128, c.TTW], F32) for i in range(2)]
    mbc = [sbuf(f"mbc{i}", [128, c.TTW], F32) for i in range(2)]

    bias_cols = sbuf("bias_cols", [128, max(c.H, 8)], F32)
    bqc = sbuf("bqc", [128, c.DC], F32)
    bkc = sbuf("bkc", [128, c.DC], F32)
    bvc = sbuf("bvc", [128, c.DC], F32)
    boc = sbuf("boc", [128, c.DC], F32)
    gqc = sbuf("gqc", [128, c.DC], F32)
    gkc = sbuf("gkc", [128, c.DC], F32)
    ident_bf = sbuf("ident_bf", [128, 128], BF16)
    Gpat = sbuf("Gpat", [128, 256], BF16)
    ones_bf = sbuf("ones_bf", [128, 1], BF16)
    ones_f = sbuf("ones_f", [128, 1], F32)
    ccol = sbuf("ccol", [128, 4], F32)
    xstg = [sbuf(f"xstg{i}", [128, c.TTW], F32) for i in range(2)]
    sqstg = [sbuf(f"sqstg{i}", [128, c.TTW], BF16) for i in range(2)]
    f32stg = [sbuf(f"f32stg{i}", [128, c.TTW], F32) for i in range(4)]
    bfstg = [sbuf(f"bfstg{i}", [128, c.TTW], BF16) for i in range(4)]
    G1 = min(c.G, 128)
    DG1 = max(c.DG_T1, 1)
    assert c.DG_T1 <= 64 and c.G <= 64
    if WBW >= c.T:
        xsum_x = wsb[0][0:G1, 0:c.T]
        xh_bf0 = wsb[1][0:c.DG_T0, 0:c.T]
        w1bf = wst[1].ap().bitcast(BF16)
        xh_bf1 = w1bf[0:DG1, 1024:1024 + c.T]
        szn_g = [w1bf[0:c.G, 128 * i:128 * (i + 1)] for i in range(2)]
        szn_u = [w1bf[0:c.G, 128 * (2 + i):128 * (3 + i)] for i in range(2)]
        sznd0 = [w1bf[0:c.DG_T0, 128 * (4 + i):128 * (5 + i)]
                 for i in range(2)]
        sznd1 = [w1bf[0:DG1, 128 * (6 + i):128 * (7 + i)] for i in range(2)]
        szs = [wst[0][0:c.G, 128 * i:128 * (i + 1)] for i in range(4)]
        szsd = [wst[0][0:c.DG_T0, 128 * (4 + i):128 * (5 + i)]
                for i in range(4)]
    else:
        xsum_x = sbuf("xsum_x", [G1, c.T], BF16)[:]
        xh_bf0 = sbuf("xh_bf0", [c.DG_T0, c.T], BF16)[:]
        xh_bf1 = sbuf("xh_bf1", [DG1, c.T], BF16)[:]
        szn_g = [sbuf(f"szn_g{i}", [c.G, 128], BF16)[:] for i in range(2)]
        szn_u = [sbuf(f"szn_u{i}", [c.G, 128], BF16)[:] for i in range(2)]
        sznd0 = [sbuf(f"sznd0_{i}", [c.DG_T0, 128], BF16)[:]
                 for i in range(2)]
        sznd1 = [sbuf(f"sznd1_{i}", [DG1, 128], BF16)[:] for i in range(2)]
        szs = [sbuf(f"szs{i}", [c.G, 128], F32)[:] for i in range(4)]
        szsd = [sbuf(f"szsd{i}", [c.DG_T0, 128], F32)[:] for i in range(4)]

    PS = ctx.enter_context(nc.psum_tensor("PS", [128, 4096], F32))

    def bank(i, p0=0, p1=128, o=0, w=512):
        return PS[p0:p1, i * 512 + o:i * 512 + o + w]

    def bankT(i):
        return PS[:, i * 512:i * 512 + c.TTW]

    def bank_bf(i, o=0, w=1024):
        return PS.ap().bitcast(BF16)[:, i * 1024 + o:i * 1024 + o + w]

    def Asl(off, n, p0=0, p1=128):
        return A[p0:p1, off:off + n]

    def Bsl(off, n, p0=0, p1=128):
        return B[p0:p1, off:off + n]

    def Bi32(off_i32, n_i32, p0=0, p1=128):
        return B.ap().bitcast(I32)[p0:p1, off_i32:off_i32 + n_i32]

    def Af32(off_f32, n_f32, p0=0, p1=128):
        return A.ap().bitcast(F32)[p0:p1, off_f32:off_f32 + n_f32]

    def Ai32(off_i32, n_i32, p0=0, p1=128):
        return A.ap().bitcast(I32)[p0:p1, off_i32:off_i32 + n_i32]

    FJW = c.FJC * 128
    _o = [0]

    def _alloc_b(n):
        off = _o[0]
        _o[0] += n
        assert _o[0] <= B_N, f"B overflow {_o[0]}>{B_N}"
        return off

    b_pk_g = _alloc_b(2 * FJW)
    b_pk_u = _alloc_b(2 * FJW)
    b_sx_g = _alloc_b(FJW)
    b_sx_u = _alloc_b(FJW)
    b_hi_g = [_alloc_b(FJW) for _ in range(2)]
    b_lo_g = [_alloc_b(FJW) for _ in range(2)]
    b_hi_u = [_alloc_b(FJW) for _ in range(2)]
    b_lo_u = [_alloc_b(FJW) for _ in range(2)]
    b_tmp6 = _alloc_b(FJW)
    b_tmp6b = _alloc_b(FJW)
    B_P6_END = _o[0]
    assert B_P6_END <= B_N

    DJW = c.DJH * 128
    _o[0] = 0
    b_pk_d = _alloc_b(2 * DJW)
    b_sx_d = [_alloc_b(DJW) for _ in range(2)]
    b_hi_d = [_alloc_b(DJW) for _ in range(3)]
    b_lo_d = [_alloc_b(DJW) for _ in range(3)]
    b_tmp7 = _alloc_b(DJW)
    B_P7_END = _o[0]
    assert B_P7_END <= B_N

    SCW = max(c.G * c.FI, c.DG * c.D) // 128
    assert 3 * SCW <= B_N, "scale staging must fit in B"

    A_xe = 0
    A_xo = c.FJC * c.T
    A_xh_f32 = (2 * c.FJC * c.T) // 2
    b_hstg = [2 * c.FJC * c.T + 4 * c.T + i * c.T for i in range(2)]
    A_sx2 = 2 * c.FJC * c.T + 6 * c.T
    assert A_sx2 + 2 * c.FJC * 128 <= A_N
    WDQ = min(11, c.DJC)
    A_wd = c.DC * c.T
    WDN = WDQ * 128
    assert A_wd + 2 * WDN + 2 * WDN + WDN + WDN + WDN <= A_N
    A_kb = [0, c.KC * 128]
    A_vb = [2 * c.KC * 128, 3 * c.KC * 128]
    A_pt = 4 * c.KC * 128
    A_he = 0
    A_ho = c.DJC * c.TTW

    sWL = E.sem("sWL")
    sWF = E.sem("sWF")
    sWB = E.sem("sWB")
    sCV = E.sem("sCV")
    sGD = E.sem("sGD")
    sSY = E.sem("sSY")
    sTM = E.sem("sTM")
    sVE = E.sem("sVE")
    sAC = E.sem("sAC")
    sPO = E.sem("sPO")
    sEX = E.sem("sEX")
    sSM = E.sem("sSM")
    sPVs = E.sem("sPVs")
    sCC = E.sem("sCC")
    sOUT = E.sem("sOUT")
    sBC = E.sem("sBC")
    sKV = E.sem("sKV")
    sWLs = [E.sem(f"sWL{i}") for i in range(2)]
    sXSs = [E.sem(f"sXS{i}") for i in range(2)]
    sBFs = [E.sem(f"sBF{i}") for i in range(4)]
    sHSs = [E.sem(f"sHS{i}") for i in range(2)]
    sVCs = [E.sem(f"sVC{i}") for i in range(2)]
    sWD = E.sem("sWD")

    ROWS = {}
    _row_off = [0]

    def row_reset():
        ROWS.clear()
        _row_off[0] = 0

    def row(name, n):
        if name not in ROWS:
            off = _row_off[0]
            _row_off[0] += n
            assert _row_off[0] <= 4160, "rowbuf overflow"
            ROWS[name] = (off, n)
        off, nn = ROWS[name]
        assert nn == n
        return rowbuf[0:1, off:off + n]

    def row_seg(name, seg_off, w):
        off, nn = ROWS[name]
        assert seg_off + w <= nn
        return rowbuf[0:1, off + seg_off:off + seg_off + w]

    marks = {}

    def prologue():
        g = "gpsimd"
        E.op(g, "memset", ident_bf[:], 0.0, inc=[(sPO, 1)])
        E.wait(g, sPO, sPO.n)
        E.op(g, "affine_select", out=ident_bf[:], in_=ident_bf[:],
             compare_op=ALU.not_equal, fill=1.0, base=0,
             pattern=[[-1, 128]], channel_multiplier=1, inc=[(sPO, 1)])
        E.op(g, "memset", Gpat[:], 0.0, inc=[(sPO, 1)])
        E.wait(g, sPO, sPO.n)
        E.op(g, "memset", Gpat[0:64, 128:129], 1.0, inc=[(sPO, 1)])
        E.op(g, "memset", Gpat[64:128, 129:130], 1.0, inc=[(sPO, 1)])
        E.op(g, "memset", ones_bf[:], 1.0, inc=[(sPO, 1)])
        E.op(g, "memset", ones_f[:], 1.0, inc=[(sPO, 1)])
        E.op(g, "memset", ccol[:, 0:1], float(c.EPS_LN), inc=[(sPO, 1)])
        E.op(g, "memset", ccol[:, 1:2], float(c.EPS_RMS), inc=[(sPO, 1)])
        E.op(g, "memset", ccol[:, 2:3], 1e-30, inc=[(sPO, 1)])
        E.op(g, "memset", ccol[:, 3:4], -0.5 * math.log(c.HD),
             inc=[(sPO, 1)])
        for tdst, src in ((bqc, bq), (bkc, bk), (bvc, bv), (boc, bo),
                          (gqc, gq), (gkc, gk)):
            E.dma(g, tdst[:], src.ap(), inc=[(sGD, 16)])
        for eng in Emitter.ENGS:
            E.wait(eng, sPO, sPO.n)
            E.wait(eng, sGD, sGD.n)

    def row_rstd(tag, sum_ap, sum2_ap, n, eps, with_mean=True):
        eps_col = {c.EPS_LN: 0, c.EPS_RMS: 1}[eps]
        E.wait("vector", sAC, sAC.n)
        E.wait("vector", sBC, sBC.n)
        v, s = "vector", "scalar"
        w = sum2_ap.shape[-1]
        ex2 = row("sh_ex2", w)
        r_r = row(tag + "_r", w)
        E.op(v, "tensor_scalar", out=ex2, in0=sum2_ap, scalar1=1.0 / n,
             scalar2=None, op0=ALU.mult, inc=[(sVE, 1)])
        if with_mean:
            mu = row("sh_mu", w)
            tsq = row("sh_tsq", w)
            m2_r = row(tag + "_m2", w)
            E.op(v, "tensor_scalar", out=mu, in0=sum_ap, scalar1=1.0 / n,
                 scalar2=None, op0=ALU.mult, inc=[(sVE, 1)])
            E.wait(v, sVE, sVE.n)
            E.op(v, "tensor_tensor", out=tsq, in0=mu, in1=mu, op=ALU.mult,
                 inc=[(sVE, 1)])
            E.wait(v, sVE, sVE.n)
            E.op(v, "tensor_tensor", out=ex2, in0=ex2, in1=tsq,
                 op=ALU.subtract, inc=[(sVE, 1)])
        lnv = row("sh_lnv", w)
        E.wait(s, sVE, sVE.n)
        E.op(s, "activation", lnv, ex2, AF.Ln,
             bias=ccol[0:1, eps_col:eps_col + 1], scale=1.0,
             inc=[(sAC, 1)])
        E.wait(s, sAC, sAC.n)
        E.op(s, "activation", r_r, lnv, AF.Exp, bias=0.0, scale=-0.5,
             inc=[(sAC, 1)])
        if with_mean:
            E.wait(v, sAC, sAC.n)
            E.op(v, "scalar_tensor_tensor", out=m2_r, in0=mu, scalar=-1.0,
                 in1=r_r, op0=ALU.mult, op1=ALU.mult, inc=[(sVE, 1)])
            return r_r, m2_r
        return r_r, None

    def bounce_rows(pairs, scr_row):
        g = "gpsimd"
        E.wait(g, sBC, sBC.n)
        E.wait(g, sVE, sVE.n)
        E.wait(g, sAC, sAC.n)
        for i, (rap, _) in enumerate(pairs):
            w = rap.shape[-1]
            E.dma(g, rbounce.ap()[scr_row + i:scr_row + i + 1, 0:w], rap,
                  inc=[(sBC, 16)])
        E.wait(g, sBC, sBC.n)
        for i, (rap, tap) in enumerate(pairs):
            w = rap.shape[-1]
            src = AP(tensor=rbounce.ap().tensor,
                     offset=(scr_row + i) * rbounce.shape[1],
                     ap=[[0, 128], [1, w]])
            E.dma(g, tap, src, inc=[(sBC, 16)])
        return sBC.n

    def emit_nibbles(pk_i32, hi_bf, lo_bf, tmp_i32, W, sx_bf=None,
                     eng="vector", hi_shift=False):
        v = eng
        sem = sPO if eng == "gpsimd" else sVE
        HW = W // 2
        for hf2 in range(2):
            a, b2 = hf2 * HW, (hf2 + 1) * HW
            for mask, dst in ((0xF0, hi_bf), (0x0F, lo_bf)):
                E.wait(v, sem, sem.n)
                if mask == 0xF0 and hi_shift:
                    E.op(v, "tensor_scalar", out=tmp_i32[:, 0:HW],
                         in0=pk_i32[:, a:b2], scalar1=0xF0, scalar2=4,
                         op0=ALU.bitwise_and,
                         op1=ALU.logical_shift_right, inc=[(sem, 1)])
                else:
                    E.op(v, "tensor_scalar", out=tmp_i32[:, 0:HW],
                         in0=pk_i32[:, a:b2], scalar1=mask, scalar2=None,
                         op0=ALU.bitwise_and, inc=[(sem, 1)])
                E.wait(v, sem, sem.n)
                if sx_bf is None:
                    E.op(v, "tensor_copy", dst[:, a:b2], tmp_i32[:, 0:HW],
                         inc=[(sem, 1)])
                else:
                    E.op(v, "tensor_tensor", out=dst[:, a:b2],
                         in0=tmp_i32[:, 0:HW], in1=sx_bf[:, a:b2],
                         op=ALU.mult, inc=[(sem, 1)])
        E.wait(v, sem, sem.n)

    wd_state = {"i": 0, "nib": {}, "ld": {}, "st": {}}

    def wd_steps_total():
        out = []
        for hc in range(c.DC):
            j0 = 0
            while j0 < c.DJC:
                j1 = min(c.DJC, j0 + WDQ)
                out.append((hc, j0, j1))
                j0 = j1
        return out

    WD_STEPS = wd_steps_total()

    def do_wd_step():
        k = wd_state["i"]
        if k >= len(WD_STEPS):
            return
        wd_state["i"] = k + 1
        hc, j0, j1 = WD_STEPS[k]
        nw = (j1 - j0) * 128
        v = "vector"
        pk = Ai32(A_wd // 2, WDN)
        sx = Af32((A_wd + 2 * WDN) // 2, WDN)
        hi = Asl(A_wd + 4 * WDN, WDN)
        lo = Asl(A_wd + 5 * WDN, WDN)
        tmp = Ai32((A_wd + 6 * WDN) // 2, WDN // 2)
        if k >= 1:
            E.wait("sync", sVE, wd_state["nib"][k - 1])
        E.dma("sync", pk[:, 0:nw].rearrange("p (jc i) -> p jc i", i=128),
              wdp.ap()[hc, :, j0:j1, :], inc=[(sSY, 16)])
        for b in range(4):
            s2 = AP(tensor=sd.ap().tensor,
                    offset=(4 * j0 + b) * c.D + hc * 128,
                    ap=[[0, 32], [4 * c.D, j1 - j0], [1, 128]])
            E.dma("sync", sx[32 * b:32 * b + 32, 0:nw].rearrange(
                "p (jc i) -> p jc i", i=128), s2, inc=[(sSY, 16)])
        ldm = sSY.n
        E.wait(v, sSY, ldm)
        if k >= 1:
            E.wait(v, sWD, wd_state["st"][k - 1])
        emit_nibbles(pk[:, 0:nw], hi[:, 0:nw], lo[:, 0:nw], tmp, nw,
                     sx_bf=sx, hi_shift=True)
        wd_state["nib"][k] = sVE.n
        g2 = "gpsimd"
        E.wait(g2, sVE, wd_state["nib"][k])
        for par, slab in ((0, hi), (1, lo)):
            dst = AP(tensor=wdeq_d.ap().tensor,
                     offset=(hc * c.FI + 2 * j0 * 128 + par) * 128,
                     ap=[[2 * 128, 128], [2 * 128 * 128, j1 - j0],
                         [1, 128]])
            E.dma(g2, dst, slab[:, 0:nw].rearrange(
                "p (jc i) -> p jc i", i=128), inc=[(sWD, 16)])
        wd_state["st"][k] = sWD.n

    wblk = [0]

    def stream_w_subblock(src_ap):
        idx = wblk[0]
        wblk[0] += 1
        sl = idx % 2
        if idx >= 2:
            E.wait("sync", sCV, idx - 1)
        E.dma("sync", wst[sl][:].rearrange("p (h i) -> p h i", i=128),
              src_ap, inc=[(sWLs[sl], 16)])
        s = "scalar"
        E.wait(s, sWLs[sl], sWLs[sl].n)
        if idx >= 2:
            E.wait(s, sWB, idx - 1)
        E.op(s, "activation", wsb[sl][:], wst[sl][:], AF.Copy,
             bias=0.0, scale=1.0, inc=[(sCV, 1)])
        return sl, sCV.n

    def p1_ln1():
        v, t, g = "vector", "tensor", "gpsimd"
        row_reset()
        xf = B.ap().bitcast(F32)
        for tt in range(c.NTT):
            t0 = tt * c.TTW
            stat_marks = {}
            for ic in range(c.DC):
                E.dma(g, xf[:, ic * c.TTW:(ic + 1) * c.TTW],
                      xT.ap()[ic * 128:(ic + 1) * 128, t0:t0 + c.TTW],
                      inc=[(sGD, 16)])
            tile_ld = sGD.n
            for ic in range(c.DC):
                sl = ic % 2
                E.wait(v, sGD, tile_ld)
                if ic >= 2:
                    E.wait(v, sTM, stat_marks[ic - 2])
                xc = xf[:, ic * c.TTW:(ic + 1) * c.TTW]
                E.op(v, "tensor_tensor", out=f32stg[sl][:], in0=xc, in1=xc,
                     op=ALU.mult, inc=[(sVE, 1)])
                E.wait(t, sGD, tile_ld)
                E.wait(t, sVE, sVE.n)
                E.mm(bank(6, 0, 1, 0, c.TTW), ones_f[:], xc,
                     start=(ic == 0), stop=(ic == c.DC - 1))
                E.mm(bank(7, 0, 1, 0, c.TTW), ones_f[:], f32stg[sl][:],
                     start=(ic == 0), stop=(ic == c.DC - 1), inc=[(sTM, 1)])
                stat_marks[ic] = sTM.n
            E.wait(v, sTM, sTM.n)
            sum_r = row("p1_sum", c.TTW)
            sum2_r = row("p1_sum2", c.TTW)
            E.op(v, "tensor_copy", sum_r, bank(6, 0, 1, 0, c.TTW),
                 inc=[(sVE, 1)])
            E.op(v, "tensor_copy", sum2_r, bank(7, 0, 1, 0, c.TTW),
                 inc=[(sVE, 1)])
            E.wait(v, sVE, sVE.n)
            r_r, m2_r = row_rstd("p1", sum_r, sum2_r, c.D, c.EPS_LN)
            bc_mark = bounce_rows([(r_r, rbc[0][:]), (m2_r, mbc[0][:])], 0)
            for ic in range(c.DC):
                do_wd_step()
                sl = 2 + ic % 2
                xc = xf[:, ic * c.TTW:(ic + 1) * c.TTW]
                E.wait(v, sBC, bc_mark)
                E.op(v, "tensor_tensor", out=f32stg[sl][:], in0=xc,
                     in1=rbc[0][:], op=ALU.mult, inc=[(sVE, 1)])
                E.wait(v, sVE, sVE.n)
                x1c = Asl(ic * c.T + t0, c.TTW)
                E.op(v, "tensor_tensor", out=x1c, in0=f32stg[sl][:],
                     in1=mbc[0][:], op=ALU.add, inc=[(sVE, 1)])
            E.wait(g, sVE, sVE.n)
        marks["x1"] = sVE.n

    def qkv_matrix_mms(w_dram, ic, banks, extra_waits):
        t = "tensor"
        hpw = c.DC // c.WSUB
        for sub in range(c.WSUB):
            src_ap = w_dram.ap()[ic][:, sub * hpw:(sub + 1) * hpw, :]
            sl, cm = stream_w_subblock(src_ap)
            E.wait(t, sCV, cm)
            if sub == 0:
                for wv_ in extra_waits:
                    E.wait(t, sVE, wv_)
            for hh in range(hpw):
                hc = sub * hpw + hh
                last_h = (sub == c.WSUB - 1 and hh == hpw - 1)
                for tt in range(c.NTT):
                    t0 = tt * c.TTW
                    sub_last = (hh == hpw - 1 and tt == c.NTT - 1)
                    E.mm(bankT(banks[tt]),
                         wsb[sl][:, hh * 128:(hh + 1) * 128],
                         Asl(hc * c.T + t0, c.TTW),
                         start=(hc == 0), stop=last_h,
                         inc=[(sWB, 1)] if sub_last else None)
        return sWB.n

    def qk_matrix(w_dram, b_col, g_col, name, x1_wait=None):
        v, t = "vector", "tensor"
        row_reset()
        ev_marks = {}
        sq_marks = {}
        sq_acm = {}

        def emit_sq(gj):
            ttj = gj % 2
            E.wait(t, sAC, sq_acm[gj])
            E.mm(bank(6 + ttj, 0, 1, 0, c.TTW), ones_bf[:],
                 sqstg[gj % 2][:], start=(gj < 2),
                 stop=(gj >= 2 * (c.DC - 1)), inc=[(sTM, 1)])
            sq_marks[gj] = sTM.n

        for ic in range(c.DC):
            do_wd_step()
            banks = [(2 * ic + tt) % 4 for tt in range(c.NTT)]
            waits = list(x1_wait) if (ic == 0 and x1_wait) else []
            for tt in range(c.NTT):
                gi = 2 * ic + tt
                if gi >= 4:
                    waits.append(ev_marks[gi - 4])
            mmk = qkv_matrix_mms(w_dram, ic, banks, waits)
            for tt in range(c.NTT):
                t0 = tt * c.TTW
                gi = 2 * ic + tt
                if gi >= 2:
                    emit_sq(gi - 2)
                E.wait(v, sWB, mmk)
                pc = Bsl(ic * c.T + t0, c.TTW)
                E.op(v, "tensor_scalar", out=pc, in0=bankT(banks[tt]),
                     scalar1=b_col[:, ic:ic + 1], scalar2=None,
                     op0=ALU.add, inc=[(sVE, 1)])
                ev_marks[gi] = sVE.n
                ssl = gi % 2
                E.wait("scalar", sVE, ev_marks[gi])
                if gi >= 2:
                    E.wait("scalar", sTM, sq_marks[gi - 2])
                E.op("scalar", "activation", sqstg[ssl][:], pc, AF.Square,
                     bias=0.0, scale=1.0, inc=[(sAC, 1)])
                sq_acm[gi] = sAC.n
        emit_sq(2 * c.DC - 2)
        emit_sq(2 * c.DC - 1)
        E.wait(v, sTM, sTM.n)
        rtiles = []
        for tt in range(c.NTT):
            sum2_r = row(f"{name}_s2_{tt}", c.TTW)
            E.op(v, "tensor_copy", sum2_r, bank(6 + tt, 0, 1, 0, c.TTW),
                 inc=[(sVE, 1)])
            E.wait(v, sVE, sVE.n)
            r_r, _ = row_rstd(f"{name}_{tt}", None, sum2_r, c.D, c.EPS_RMS,
                              with_mean=False)
            rtiles.append((r_r, rbc[tt][:]))
        bc_mark = bounce_rows(rtiles, 2)
        for ic in range(c.DC):
            for tt in range(c.NTT):
                t0 = tt * c.TTW
                pc = Bsl(ic * c.T + t0, c.TTW)
                E.wait(v, sBC, bc_mark)
                E.op(v, "scalar_tensor_tensor", out=pc, in0=pc,
                     scalar=g_col[:, ic:ic + 1], in1=rbc[tt][:],
                     op0=ALU.mult, op1=ALU.mult, inc=[(sVE, 1)])
        return sVE.n

    def p2_qkv():
        v, s, t, g = "vector", "scalar", "tensor", "gpsimd"
        kmark = qk_matrix(wk, bkc, gkc, "p2k",
                          x1_wait=[marks["x1"]])
        E.wait(v, sVE, kmark)
        E.wait(g, sVE, kmark)
        for ic in range(c.DC):
            E.dma(g, kc_d.ap()[ic * 128:(ic + 1) * 128, :],
                  Bsl(ic * c.T, c.T), inc=[(sGD, 16)])
        knloc = row("knloc", max(c.H, 8))
        for h in range(c.H):
            for tt in range(c.NTT):
                ssl = tt % 2
                E.wait(v, sTM, sTM.n)
                kc_ap = Bsl(h * c.T + tt * c.TTW, c.TTW)
                E.op(v, "tensor_tensor", out=sqstg[ssl][:], in0=kc_ap,
                     in1=kc_ap, op=ALU.mult, inc=[(sVE, 1)])
                E.wait(t, sVE, sVE.n)
                E.mm(bank(4 + tt, 0, 1, 0, c.TTW), ones_bf[:], sqstg[ssl][:],
                     start=True, stop=True, inc=[(sTM, 1)])
            E.wait(v, sTM, sTM.n)
            knr = row("knr", c.T)
            for tt in range(c.NTT):
                E.op(v, "tensor_copy", row_seg("knr", tt * c.TTW, c.TTW),
                     bank(4 + tt, 0, 1, 0, c.TTW), inc=[(sVE, 1)])
            E.wait(v, sVE, sVE.n)
            E.op(v, "reduce_max", knloc[0:1, h:h + 1], knr, axis=AXX,
                 inc=[(sVE, 1)])
            E.wait(v, sVE, sVE.n)
        E.op(v, "tensor_copy", sqstg[0][0:1, 0:c.H], knloc[0:1, 0:c.H],
             inc=[(sVE, 1)])
        E.op(v, "memset", sqstg[1][0:1, :], 0.0, inc=[(sVE, 1)])
        E.wait(g, sVE, sVE.n)
        E.dma(g, kc_d.ap()[c.D:c.D + 1, 0:c.H], sqstg[0][0:1, 0:c.H],
              inc=[(sGD, 16)])
        off = c.H
        while off < c.T:
            wseg = min(c.TTW, c.T - off)
            E.dma(g, kc_d.ap()[c.D:c.D + 1, off:off + wseg],
                  sqstg[1][0:1, 0:wseg], inc=[(sGD, 16)])
            off += wseg
        marks["contrib"] = sGD.n

        E.wait(g, sGD, marks["contrib"])
        E.op(g, "collective_compute", "AllGather", ALU.bypass,
             replica_groups=[list(range(NCORES))],
             ins=[kc_d.ap().opt()], outs=[kall.ap().opt()],
             inc=[(sCC, 1)])

        row_reset()
        tr_bank_mark = {2: None, 3: None}
        vev_marks = {}
        vst_marks = {}
        for ic in range(c.DC):
            do_wd_step()
            banks = [(2 * ic + tt) % 2 for tt in range(c.NTT)]
            waits = []
            for tt in range(c.NTT):
                gi = 2 * ic + tt
                if gi >= 2:
                    waits.append(vev_marks[gi - 2])
            mmk = qkv_matrix_mms(wv, ic, banks, waits)
            for tt in range(c.NTT):
                t0 = tt * c.TTW
                gi = 2 * ic + tt
                E.wait(v, sWB, mmk)
                vp = bfstg[gi % 2][:]
                if gi >= 2:
                    E.wait(v, sTM, vst_marks[gi - 2])
                E.op(v, "tensor_scalar", out=vp, in0=bankT(banks[tt]),
                     scalar1=bvc[:, ic:ic + 1], scalar2=None, op0=ALU.add,
                     inc=[(sVE, 1)])
                vev_marks[gi] = sVE.n
                tb = 2 + gi % 2
                ntile = c.TTW // 128
                E.wait(t, sVE, sVE.n)
                if tr_bank_mark[tb] is not None:
                    E.wait(t, sVE, tr_bank_mark[tb])
                for tl in range(ntile):
                    E.op(t, "transpose", bank_bf(tb, tl * 128, 128),
                         vp[:, tl * 128:(tl + 1) * 128], ident_bf[:],
                         inc=[(sTM, 1)] if tl == ntile - 1 else None)
                vst_marks[gi] = sTM.n
                E.wait(v, sTM, sTM.n)
                if gi >= 2:
                    E.wait(v, sVCs[gi % 2], sVCs[gi % 2].n)
                vt = bfstg[2 + gi % 2][:]
                E.op(v, "tensor_copy", vt[:, 0:ntile * 128],
                     bank_bf(tb, 0, ntile * 128), inc=[(sVE, 1)])
                tr_bank_mark[tb] = sVE.n
                E.wait(g, sVE, sVE.n)
                for tl in range(ntile):
                    tc = t0 // 128 + tl
                    E.dma(g, vc_d.ap()[ic * c.T + tc * 128:
                                       ic * c.T + (tc + 1) * 128, :],
                          vt[:, tl * 128:(tl + 1) * 128],
                          inc=[(sVCs[gi % 2], 16)])

        marks["v_done"] = sVE.n
        E.wait(g, sVCs[0], sVCs[0].n)
        E.wait(g, sVCs[1], sVCs[1].n)
        E.op(g, "collective_compute", "AllGather", ALU.bypass,
             replica_groups=[list(range(NCORES))],
             ins=[vc_d.ap().opt()], outs=[vall.ap().opt()],
             inc=[(sCC, 1)])

        E.wait(v, sGD, marks["contrib"])
        marks["q_ready"] = qk_matrix(wq, bqc, gqc, "p2q",
                                     x1_wait=[marks["v_done"]])

    def p3_collect():
        v, s, t, g = "vector", "scalar", "tensor", "gpsimd"
        row_reset()
        marks["cc"] = sCC.n
        while wd_state["i"] < len(WD_STEPS):
            do_wd_step()
        marks["wd_done"] = sWD.n
        E.wait(g, sCC, sCC.n)
        E.wait(v, sVE, marks["q_ready"])
        mq2 = row("mq2", max(c.H, 8))
        for h in range(c.H):
            for tt in range(c.NTT):
                ssl = tt % 2
                E.wait(v, sTM, sTM.n)
                qc_ap = Bsl(h * c.T + tt * c.TTW, c.TTW)
                E.op(v, "tensor_tensor", out=sqstg[ssl][:], in0=qc_ap,
                     in1=qc_ap, op=ALU.mult, inc=[(sVE, 1)])
                E.wait(t, sVE, sVE.n)
                E.mm(bank(4 + tt, 0, 1, 0, c.TTW), ones_bf[:], sqstg[ssl][:],
                     start=True, stop=True, inc=[(sTM, 1)])
            E.wait(v, sTM, sTM.n)
            qnr = row("qnr", c.T)
            for tt in range(c.NTT):
                E.op(v, "tensor_copy", row_seg("qnr", tt * c.TTW, c.TTW),
                     bank(4 + tt, 0, 1, 0, c.TTW), inc=[(sVE, 1)])
            E.wait(v, sVE, sVE.n)
            E.op(v, "reduce_max", mq2[0:1, h:h + 1], qnr, axis=AXX,
                 inc=[(sVE, 1)])
            E.wait(v, sVE, sVE.n)
        E.wait(g, sVE, sVE.n)
        E.wait(g, sCC, marks["cc"])
        for h in range(c.H):
            src = AP(tensor=kall.ap().tensor, offset=c.D * c.T + h,
                     ap=[[(c.D + 1) * c.T, NCORES], [1, 1]])
            E.dma(g, sqstg[0][0:1, h * NCORES:(h + 1) * NCORES].rearrange(
                "p (r one) -> p r one", r=NCORES), src, inc=[(sGD, 16)],
                allow_slow_non_contiguous=True)
        E.wait(v, sGD, sGD.n)
        mk2 = row("mk2", max(c.H, 8))
        for h in range(c.H):
            E.op(v, "reduce_max", mk2[0:1, h:h + 1],
                 sqstg[0][0:1, h * NCORES:(h + 1) * NCORES], axis=AXX,
                 inc=[(sVE, 1)])
        E.wait(v, sVE, sVE.n)
        prod = row("bias_prod", max(c.H, 8))
        E.op(v, "tensor_tensor", out=prod, in0=mq2, in1=mk2, op=ALU.mult,
             inc=[(sVE, 1)])
        E.wait(s, sVE, sVE.n)
        lnp = row("bias_ln", max(c.H, 8))
        E.op(s, "activation", lnp, prod, AF.Ln, bias=ccol[0:1, 2:3],
             scale=1.0, inc=[(sAC, 1)])
        E.wait(s, sAC, sAC.n)
        cs_r = row("bias_cs", max(c.H, 8))
        E.op(s, "activation", cs_r, lnp, AF.Exp, bias=ccol[0:1, 3:4],
             scale=0.5, inc=[(sAC, 1)])
        E.wait(v, sAC, sAC.n)
        negc = row("bias_negc", max(c.H, 8))
        E.op(v, "tensor_scalar", out=negc, in0=cs_r, scalar1=-1.0,
             scalar2=CS_OFF, op0=ALU.mult, op1=ALU.add, inc=[(sVE, 1)])
        E.wait(g, sVE, sVE.n)
        E.wait(g, sBC, sBC.n)
        E.dma(g, rbounce.ap()[6:7, 0:max(c.H, 8)], negc, inc=[(sBC, 16)])
        E.wait(g, sBC, sBC.n)
        for h in range(c.H):
            src = AP(tensor=rbounce.ap().tensor,
                     offset=6 * rbounce.shape[1] + h,
                     ap=[[0, 128], [1, 1]])
            E.dma(g, bias_cols[:, h:h + 1], src, inc=[(sBC, 16)])
        marks["bias_cols"] = sBC.n

    def p4_attention():
        v, s, t, g = "vector", "scalar", "tensor", "gpsimd"
        row_reset()
        KC = c.KC
        A_den_f32 = (4 * c.KC * 128 + NPT * c.T) // 2
        den_acc = Af32(A_den_f32, c.T)
        kv_ld = {}
        pv_done_mark = {}
        oe_marks = {}
        exp_mark = {}
        pv_mark = {}
        den_mark = {}

        def load_head(h):
            slb = h % 2
            E.wait(g, sCC, marks["cc"])
            E.wait(g, sVE, marks["q_ready"])
            E.wait(g, sKV, sKV.n)
            if h >= 2:
                E.wait(g, sPVs, pv_done_mark[h - 2])
            for r in range(NCORES):
                E.dma(g, Asl(A_kb[slb] + r * c.T, c.T),
                      kall.ap()[r * (c.D + 1) + h * 128:
                                r * (c.D + 1) + (h + 1) * 128, 0:c.T],
                      inc=[(sKV, 16)])
            for r in range(NCORES):
                src = AP(tensor=vall.ap().tensor,
                         offset=(r * c.H * c.T + h * c.T) * c.HD,
                         ap=[[c.HD, 128], [128 * c.HD, c.T // 128],
                             [1, c.HD]])
                E.dma(g, Asl(A_vb[slb] + r * c.T, c.T).rearrange(
                    "p (cc d) -> p cc d", d=c.HD), src, inc=[(sKV, 16)])
            kv_ld[h] = sKV.n

        for eng in ("scalar", "tensor", "vector", "gpsimd"):
            E.wait(eng, sWD, marks["wd_done"])
        load_head(0)
        if c.H > 1:
            load_head(1)
        E.wait(s, sBC, marks["bias_cols"])
        for h in range(c.H):
            slb = h % 2
            E.wait(v, sVE, sVE.n)
            E.wait(v, sTM, sTM.n)
            E.op(v, "memset", den_acc, 0.0, inc=[(sVE, 1)])
            E.wait(t, sKV, kv_ld[h])
            E.wait(t, sVE, marks["q_ready"])
            if h >= 1:
                E.wait(t, sVE, oe_marks[h - 1])
            def emit_pv(kc):
                pslot = kc % NPT
                E.wait(t, sEX, exp_mark[(h, kc)])
                vch = Asl(A_vb[slb] + kc * 128, 128)
                E.mm(bank(6, 0, 1, 0, c.TTW), ones_bf[:],
                     Asl(A_pt + pslot * c.T, c.TTW),
                     start=(kc == 0), stop=(kc == KC - 1))
                for qt in range(c.NTT):
                    q0 = qt * c.TTW
                    E.mm(bank(4 + qt, 0, 128, 0, c.TTW), vch,
                         Asl(A_pt + pslot * c.T + q0, c.TTW),
                         start=(kc == 0), stop=(kc == KC - 1),
                         inc=[(sPVs, 1)] if qt == c.NTT - 1 else None)
                pv_mark[(h, kc)] = sPVs.n
                E.wait(v, sEX, exp_mark[(h, kc)])
                E.wait(v, sVE, sVE.n)
                E.op(v, "tensor_tensor",
                     out=den_acc[:, c.TTW:c.T], in0=den_acc[:, c.TTW:c.T],
                     in1=Asl(A_pt + pslot * c.T + c.TTW, c.T - c.TTW),
                     op=ALU.add, inc=[(sVE, 1)])
                den_mark[(h, kc)] = sVE.n

            for kc in range(KC):
                grp = kc % 2
                if (h, kc - 2) in exp_mark:
                    E.wait(t, sEX, exp_mark[(h, kc - 2)])
                elif kc < 2 and (h - 1, KC - 2 + kc) in exp_mark:
                    E.wait(t, sEX, exp_mark[(h - 1, KC - 2 + kc)])
                lhs = Asl(A_kb[slb] + kc * 128, 128)
                for qt in range(c.NTT):
                    q0 = qt * c.TTW
                    E.mm(bank(grp * 2 + q0 // 512, 0, 128, q0 % 512, c.TTW),
                         lhs, Bsl(h * c.T + q0, c.TTW),
                         start=True, stop=True,
                         inc=[(sSM, 1)] if qt == c.NTT - 1 else None)
                sm_mark = sSM.n
                pslot = kc % NPT
                E.wait(s, sSM, sm_mark)
                if (h, kc - NPT) in pv_mark:
                    E.wait(s, sPVs, pv_mark[(h, kc - NPT)])
                    E.wait(s, sVE, den_mark[(h, kc - NPT)])
                elif kc < NPT and (h - 1, KC - NPT + kc) in pv_mark:
                    E.wait(s, sPVs, pv_mark[(h - 1, KC - NPT + kc)])
                    E.wait(s, sVE, den_mark[(h - 1, KC - NPT + kc)])
                E.op(s, "activation", Asl(A_pt + pslot * c.T, c.T),
                     PS[:, grp * 1024:grp * 1024 + c.T], AF.Exp,
                     bias=bias_cols[:, h:h + 1], scale=RSQ, inc=[(sEX, 1)])
                exp_mark[(h, kc)] = sEX.n
                if kc >= 1:
                    emit_pv(kc - 1)
            emit_pv(KC - 1)
            pv_done_mark[h] = sPVs.n
            if h + 2 < c.H:
                load_head(h + 2)
            E.wait(v, sVE, den_mark[(h, KC - 1)])
            if h >= 1:
                E.wait(v, sVE, marks.get("nrm_%d" % (h - 1), 0))
            den_r = row("den_r", c.T)
            E.wait(t, sVE, sVE.n)
            for qt in range(1, c.NTT):
                E.mm(bank(6 + qt, 0, 1, 0, c.TTW), ones_f[:],
                     den_acc[:, qt * c.TTW:(qt + 1) * c.TTW],
                     start=True, stop=True, inc=[(sTM, 1)])
            E.wait(v, sTM, sTM.n)
            E.wait(v, sPVs, pv_done_mark[h])
            for qt in range(c.NTT):
                q0 = qt * c.TTW
                E.op(v, "tensor_copy", row_seg("den_r", q0, c.TTW),
                     bank(6 + qt, 0, 1, 0, c.TTW), inc=[(sVE, 1)])
            E.wait(v, sVE, sVE.n)
            rden_r = row("rden_r", c.T)
            E.op(v, "reciprocal", rden_r, den_r, inc=[(sVE, 1)])
            bmark = bounce_rows(
                [(rowbuf[0:1, ROWS["rden_r"][0] + qt * c.TTW:
                         ROWS["rden_r"][0] + (qt + 1) * c.TTW],
                  f32stg[qt][:]) for qt in range(c.NTT)], 6)
            for qt in range(c.NTT):
                q0 = qt * c.TTW
                E.op(v, "tensor_copy", Bsl(h * c.T + q0, c.TTW),
                     bank(4 + qt, 0, 128, 0, c.TTW), inc=[(sVE, 1)])
            oe_marks[h] = sVE.n
            E.wait(v, sVE, oe_marks[h])
            E.wait(v, sBC, bmark)
            for qt in range(c.NTT):
                q0 = qt * c.TTW
                oc = Bsl(h * c.T + q0, c.TTW)
                E.op(v, "tensor_tensor", out=oc, in0=oc,
                     in1=f32stg[qt][:], op=ALU.mult,
                     inc=[(sVE, 1)])
            marks["nrm_%d" % h] = sVE.n
        marks["attn_done"] = sVE.n

    def p5_wo_ln():
        v, s, t, g = "vector", "scalar", "tensor", "gpsimd"
        row_reset()
        ev_marks = {}
        wo_sq = {}
        sq_acm = {}

        def emit_wo_stats(gj):
            ttj = gj % 2
            dmj = gj // 2
            t0j = ttj * c.TTW
            aocj = Asl(dmj * c.T + t0j, c.TTW)
            E.wait(t, sVE, ev_marks[gj])
            E.wait(t, sAC, sq_acm[gj])
            E.mm(bank(6 + ttj, 0, 1, 0, c.TTW), ones_bf[:], aocj,
                 start=(gj < 2), stop=(gj >= 2 * (c.DC - 1)))
            E.mm(bank(4 + ttj, 0, 1, 0, c.TTW), ones_bf[:],
                 sqstg[gj % 2][:], start=(gj < 2),
                 stop=(gj >= 2 * (c.DC - 1)), inc=[(sTM, 1)])
            wo_sq[gj] = sTM.n

        for dm in range(c.DC):
            banks = [(2 * dm + tt) % 4 for tt in range(c.NTT)]
            hpw = c.DC // c.WSUB
            for sub in range(c.WSUB):
                src_ap = wo.ap()[dm][:, sub * hpw:(sub + 1) * hpw, :]
                sl, cm = stream_w_subblock(src_ap)
                E.wait(t, sCV, cm)
                if dm == 0 and sub == 0:
                    E.wait(t, sVE, marks["attn_done"])
                if sub == 0:
                    for tt in range(c.NTT):
                        gi = 2 * dm + tt
                        if gi >= 4:
                            E.wait(t, sVE, ev_marks[gi - 4])
                for hh in range(hpw):
                    ic = sub * hpw + hh
                    last_h = (sub == c.WSUB - 1 and hh == hpw - 1)
                    for tt in range(c.NTT):
                        t0 = tt * c.TTW
                        sub_last = (hh == hpw - 1 and tt == c.NTT - 1)
                        E.mm(bankT(banks[tt]),
                             wsb[sl][:, hh * 128:(hh + 1) * 128],
                             Bsl(ic * c.T + t0, c.TTW),
                             start=(ic == 0), stop=last_h,
                             inc=[(sWB, 1)] if sub_last else None)
            mm_mark = sWB.n
            for tt in range(c.NTT):
                t0 = tt * c.TTW
                gi = 2 * dm + tt
                if gi >= 2:
                    emit_wo_stats(gi - 2)
                E.wait(v, sWB, mm_mark)
                aoc = Asl(dm * c.T + t0, c.TTW)
                E.op(v, "tensor_scalar", out=aoc, in0=bankT(banks[tt]),
                     scalar1=boc[:, dm:dm + 1], scalar2=None, op0=ALU.add,
                     inc=[(sVE, 1)])
                ev_marks[gi] = sVE.n
                ssl = gi % 2
                E.wait("scalar", sVE, ev_marks[gi])
                if gi >= 2:
                    E.wait("scalar", sTM, wo_sq.get(gi - 2, 0))
                E.op("scalar", "activation", sqstg[ssl][:], aoc, AF.Square,
                     bias=0.0, scale=1.0, inc=[(sAC, 1)])
                sq_acm[gi] = sAC.n
        emit_wo_stats(2 * c.DC - 2)
        emit_wo_stats(2 * c.DC - 1)
        marks["wo_done"] = sWB.n
        E.wait(v, sTM, sTM.n)
        for tt in range(c.NTT):
            t0 = tt * c.TTW
            row_reset()
            sum_r = row("p5_sum", c.TTW)
            sum2_r = row("p5_sum2", c.TTW)
            E.op(v, "tensor_copy", sum_r, bank(6 + tt, 0, 1, 0, c.TTW),
                 inc=[(sVE, 1)])
            E.op(v, "tensor_copy", sum2_r, bank(4 + tt, 0, 1, 0, c.TTW),
                 inc=[(sVE, 1)])
            E.wait(v, sVE, sVE.n)
            r_r, m2_r = row_rstd("p5", sum_r, sum2_r, c.D, c.EPS_LN)
            bc_mark = bounce_rows([(r_r, rbc[0][:]), (m2_r, mbc[0][:])], 0)
            xm_marks = {}
            st2_marks = {}
            Bxm = B.ap().bitcast(F32)
            for ic in range(c.DC):
                sl2 = ic % 2
                if ic >= 2:
                    E.wait(g, sVE, xm_marks[ic - 2])
                E.dma(g, xstg[sl2][:],
                      xT.ap()[ic * 128:(ic + 1) * 128, t0:t0 + c.TTW],
                      inc=[(sXSs[sl2], 16)])
                ld_mark = sXSs[sl2].n
                aoc = Asl(ic * c.T + t0, c.TTW)
                xmc = Bxm[:, ic * c.TTW:(ic + 1) * c.TTW]
                E.wait(v, sXSs[sl2], ld_mark)
                E.wait(v, sBC, bc_mark)
                if ic == 0 and tt == 0:
                    E.wait(v, sWB, marks["wo_done"])
                if ic == 0 and tt > 0:
                    E.wait(v, sVE, marks.get("x2prev", 0))
                if ic >= 2:
                    E.wait(v, sTM, st2_marks[ic - 2])
                E.op(v, "tensor_tensor", out=f32stg[sl2][:], in0=aoc,
                     in1=rbc[0][:], op=ALU.mult, inc=[(sVE, 1)])
                E.wait(v, sVE, sVE.n)
                E.op(v, "tensor_tensor", out=f32stg[sl2][:],
                     in0=f32stg[sl2][:], in1=mbc[0][:], op=ALU.add,
                     inc=[(sVE, 1)])
                E.wait(v, sVE, sVE.n)
                E.op(v, "tensor_tensor", out=xmc,
                     in0=f32stg[sl2][:], in1=xstg[sl2][:], op=ALU.add,
                     inc=[(sVE, 1)])
                xm_marks[ic] = sVE.n
                E.wait(g, sVE, xm_marks[ic])
                E.dma(g, xmid_d.ap()[ic * 128:(ic + 1) * 128, t0:t0 + c.TTW],
                      xmc, inc=[(sGD, 16)])
                E.wait(v, sVE, xm_marks[ic])
                E.op(v, "tensor_tensor", out=f32stg[sl2][:],
                     in0=xmc, in1=xmc, op=ALU.mult, inc=[(sVE, 1)])
                E.wait(t, sVE, sVE.n)
                E.mm(bank(6 + tt, 0, 1, 0, c.TTW), ones_f[:], xmc,
                     start=(ic == 0), stop=(ic == c.DC - 1))
                E.mm(bank(4 + tt, 0, 1, 0, c.TTW), ones_f[:], f32stg[sl2][:],
                     start=(ic == 0), stop=(ic == c.DC - 1), inc=[(sTM, 1)])
                st2_marks[ic] = sTM.n
            marks[f"xm_done_{tt}"] = sGD.n
            E.wait(v, sTM, sTM.n)
            row_reset()
            sumb_r = row("p5b_sum", c.TTW)
            sumb2_r = row("p5b_sum2", c.TTW)
            E.op(v, "tensor_copy", sumb_r, bank(6 + tt, 0, 1, 0, c.TTW),
                 inc=[(sVE, 1)])
            E.op(v, "tensor_copy", sumb2_r, bank(4 + tt, 0, 1, 0, c.TTW),
                 inc=[(sVE, 1)])
            E.wait(v, sVE, sVE.n)
            r2_r, m22_r = row_rstd("p5b", sumb_r, sumb2_r, c.D, c.EPS_LN)
            bc2_mark = bounce_rows([(r2_r, rbc[1][:]), (m22_r, mbc[1][:])], 3)
            x2_marks = {}
            x2st_marks = {}
            Bxm = B.ap().bitcast(F32)
            for ic in range(c.DC):
                sl2 = ic % 2
                xmc = Bxm[:, ic * c.TTW:(ic + 1) * c.TTW]
                E.wait(v, sBC, bc2_mark)
                E.op(v, "tensor_tensor", out=f32stg[sl2][:], in0=xmc,
                     in1=rbc[1][:], op=ALU.mult, inc=[(sVE, 1)])
                E.wait(v, sVE, sVE.n)
                if ic >= 1:
                    E.wait(v, sGD, x2st_marks[ic - 1])
                E.op(v, "tensor_tensor", out=bfstg[sl2][:],
                     in0=f32stg[sl2][:], in1=mbc[1][:], op=ALU.add,
                     inc=[(sVE, 1)])
                x2_marks[ic] = sVE.n
                E.wait(g, sVE, x2_marks[ic])
                E.dma(g, x2_d.ap()[ic * 128:(ic + 1) * 128, t0:t0 + c.TTW],
                      bfstg[sl2][:], inc=[(sGD, 16)])
                x2st_marks[ic] = sGD.n
            marks["x2prev"] = sVE.n
        marks["x2_done"] = sGD.n

    def p5b_scales_and_gathers():
        v, t, g = "vector", "tensor", "gpsimd"
        row_reset()
        E.wait(g, sGD, marks["x2_done"])
        Bf = B.ap().bitcast(F32)
        for src, dst, rows_, cols_ in ((sg, sgbf_d, c.G, c.FI),
                                       (su, subf_d, c.G, c.FI)):
            n = rows_ * cols_
            w = n // 128
            E.wait(g, sVE, sVE.n)
            E.dma(g, Bf[:, 0:w],
                  src.ap().rearrange("a b -> (a b)").rearrange(
                      "(p w) -> p w", p=128), inc=[(sGD, 16)])
            E.wait(v, sGD, sGD.n)
            E.op(v, "tensor_copy", B[:, 2 * SCW:2 * SCW + w], Bf[:, 0:w],
                 inc=[(sVE, 1)])
            E.wait(g, sVE, sVE.n)
            E.dma(g, dst.ap().rearrange("a b -> (a b)").rearrange(
                "(p w) -> p w", p=128), B[:, 2 * SCW:2 * SCW + w],
                inc=[(sGD, 16)])
            E.wait(g, sGD, sGD.n)
        marks["scbf"] = sGD.n
        for jj in range(c.FJC):
            for hf in range(2):
                src = AP(tensor=x2_d.ap().tensor,
                         offset=(2 * jj + hf) * 128 * c.T,
                         ap=[[2 * c.T, 64], [1, c.T]])
                E.dma(g, Asl(A_xe + jj * c.T, c.T, 64 * hf, 64 * hf + 64),
                      src, inc=[(sGD, 16)])
                src = AP(tensor=x2_d.ap().tensor,
                         offset=(2 * jj + hf) * 128 * c.T + c.T,
                         ap=[[2 * c.T, 64], [1, c.T]])
                E.dma(g, Asl(A_xo + jj * c.T, c.T, 64 * hf, 64 * hf + 64),
                      src, inc=[(sGD, 16)])
        E.wait(v, sGD, sGD.n)
        E.op(v, "tensor_scalar", out=Asl(A_xe, c.FJC * c.T),
             in0=Asl(A_xe, c.FJC * c.T), scalar1=1.0 / 16.0, scalar2=None,
             op0=ALU.mult, inc=[(sVE, 1)])
        marks["xe_ready"] = sVE.n
        xsk = {}
        for tt in range(c.NTT):
            t0 = tt * c.TTW
            for icd in range(c.DC):
                gidx = tt * c.DC + icd
                sl4 = gidx % 4
                if gidx >= 4:
                    E.wait(g, sTM, xsk[gidx - 4])
                E.dma(g, bfstg[sl4][:],
                      x2_d.ap()[icd * 128:(icd + 1) * 128, t0:t0 + c.TTW],
                      inc=[(sBFs[sl4], 16)])
                E.wait(t, sBFs[sl4], sBFs[sl4].n)
                if icd == 0 and tt > 0:
                    E.wait(t, sVE, marks["xsum_cp"])
                m = icd % 64
                E.mm(bank(0, 0, min(c.G, 128), 0, c.TTW),
                     Gpat[:, 128 - 2 * m:128 - 2 * m + min(c.G, 128)],
                     bfstg[sl4][:], start=(icd == 0),
                     stop=(icd == c.DC - 1), inc=[(sTM, 1)])
                xsk[gidx] = sTM.n
            E.wait(v, sTM, sTM.n)
            E.op(v, "tensor_copy", xsum_x[:, t0:t0 + c.TTW],
                 bank(0, 0, min(c.G, 128), 0, c.TTW), inc=[(sVE, 1)])
            marks["xsum_cp"] = sVE.n
        marks["xsum_x"] = sVE.n

    def p6_gate_up():
        v, s, t, g = "vector", "scalar", "tensor", "gpsimd"
        row_reset()
        xh0 = Af32(A_xh_f32, c.T, 0, c.DG_T0)
        xh1 = Af32(A_xh_f32 + c.T, c.T, 0, max(c.DG_T1, 1))
        E.wait(v, sVE, marks["xe_ready"])
        E.op(v, "memset", xh0, 0.0, inc=[(sVE, 1)])
        if c.DG_T1 > 0:
            E.op(v, "memset", xh1, 0.0, inc=[(sVE, 1)])
        E.wait(t, sVE, marks["xe_ready"])
        E.wait(t, sVE, marks["xsum_x"])
        E.wait(s, sVE, marks["xe_ready"])
        E.wait("sync", sVE, marks["xsum_x"])
        dqm = {}
        dqmp = {}
        mmdone = {}
        hdone = {}
        hst_marks = {}
        xh_marks = {}
        xhmm_marks = {}
        for ic in range(c.FIC):
            sl = ic % 2
            pkg = Bi32(b_pk_g // 2, FJW)
            pku = Bi32(b_pk_u // 2, FJW)
            if ic % 2 == 0:
                sxg = Bsl(b_sx_g, FJW)
                sxu = Bsl(b_sx_u, FJW)
            else:
                sxg = Asl(A_sx2, FJW)
                sxu = Asl(A_sx2 + FJW, FJW)
            hig = Bsl(b_hi_g[sl], FJW)
            log_ = Bsl(b_lo_g[sl], FJW)
            hiu = Bsl(b_hi_u[sl], FJW)
            lou = Bsl(b_lo_u[sl], FJW)
            if ic >= 1:
                E.wait("sync", sVE, dqm[ic - 1])
            E.dma("sync", pkg.rearrange("p (jc i) -> p jc i", i=128),
                  wgp.ap()[ic], inc=[(sSY, 16)])
            E.dma("sync", pku.rearrange("p (jc i) -> p jc i", i=128),
                  wup.ap()[ic], inc=[(sSY, 16)])
            pk_mark = sSY.n
            if ic == 0:
                E.wait("sync", sGD, marks["scbf"])
            for b in range(4):
                src = AP(tensor=sgbf_d.ap().tensor,
                         offset=b * c.FI + ic * 128,
                         ap=[[0, 32], [4 * c.FI, c.FJC], [1, 128]])
                E.dma("sync", sxg[32 * b:32 * b + 32, :].rearrange(
                    "p (jc i) -> p jc i", i=128), src, inc=[(sSY, 16)])
                src = AP(tensor=subf_d.ap().tensor,
                         offset=b * c.FI + ic * 128,
                         ap=[[0, 32], [4 * c.FI, c.FJC], [1, 128]])
                E.dma("sync", sxu[32 * b:32 * b + 32, :].rearrange(
                    "p (jc i) -> p jc i", i=128), src, inc=[(sSY, 16)])
            E.dma("sync", szs[0],
                  sg.ap()[:, ic * 128:(ic + 1) * 128], inc=[(sSY, 16)])
            E.dma("sync", szs[1],
                  zg.ap()[:, ic * 128:(ic + 1) * 128], inc=[(sSY, 16)])
            E.dma("sync", szs[2],
                  su.ap()[:, ic * 128:(ic + 1) * 128], inc=[(sSY, 16)])
            E.dma("sync", szs[3],
                  zu.ap()[:, ic * 128:(ic + 1) * 128], inc=[(sSY, 16)])
            ld_mark = sSY.n
            E.wait(v, sSY, ld_mark)
            E.wait(g, sSY, ld_mark)
            if ic >= 2:
                E.wait(v, sTM, mmdone[ic - 2])
                E.wait(g, sTM, mmdone[ic - 2])
            tmp6 = Bi32(b_tmp6 // 2, FJW // 2)
            tmp6b = Bi32(b_tmp6b // 2, FJW // 2)
            emit_nibbles(pkg, hig, log_, tmp6, FJW, sx_bf=sxg)
            emit_nibbles(pku, hiu, lou, tmp6b, FJW, sx_bf=sxu)
            E.op(v, "scalar_tensor_tensor", out=szn_g[sl],
                 in0=szs[0], scalar=-1.0, in1=szs[1],
                 op0=ALU.mult, op1=ALU.mult, inc=[(sVE, 1)])
            E.op(v, "scalar_tensor_tensor", out=szn_u[sl],
                 in0=szs[2], scalar=-1.0, in1=szs[3],
                 op0=ALU.mult, op1=ALU.mult, inc=[(sVE, 1)])
            dqm[ic] = sVE.n
            dqmp[ic] = sPO.n
            def emit_xh(icp):
                for ttp in range(c.NTT):
                    t0p = ttp * c.TTW
                    bkgp = (icp % 2) * 4 + 2 * ttp
                    E.wait(t, sVE, hdone[(icp, ttp)])
                    m = icp % 64
                    tile1 = (2 * icp) >= 128
                    gw = max(c.DG_T1 if tile1 else c.DG_T0, 1)
                    E.mm(bank(bkgp, 0, gw, 0, c.TTW),
                         Gpat[:, 128 - 2 * m:128 - 2 * m + gw],
                         Asl(b_hstg[icp % 2] + t0p, c.TTW),
                         start=True, stop=True, inc=[(sTM, 1)])
                    xhmm_marks[(icp, ttp)] = sTM.n
                    E.wait(v, sTM, sTM.n)
                    xsb = xh1 if tile1 else xh0
                    E.op(v, "tensor_tensor",
                         out=xsb[:, t0p:t0p + c.TTW],
                         in0=xsb[:, t0p:t0p + c.TTW],
                         in1=bank(bkgp, 0, gw, 0, c.TTW), op=ALU.add,
                         inc=[(sVE, 1)])
                    xh_marks[(icp, ttp)] = sVE.n

            if ic >= 1:
                emit_xh(ic - 1)
            E.wait(t, sVE, dqm[ic])
            E.wait(t, sPO, dqmp[ic])
            for tt in range(c.NTT):
                t0 = tt * c.TTW
                bkg = (ic % 2) * 4 + 2 * tt
                bku = bkg + 1
                if ic >= 2:
                    E.wait(t, sVE, hdone[(ic - 2, tt)])
                    E.wait(t, sVE, xh_marks[(ic - 2, tt)])
                for jc in range(c.FJC):
                    E.mm(bankT(bkg), hig[:, jc * 128:(jc + 1) * 128],
                         Asl(A_xe + jc * c.T + t0, c.TTW),
                         start=(jc == 0), stop=False)
                    E.mm(bankT(bkg), log_[:, jc * 128:(jc + 1) * 128],
                         Asl(A_xo + jc * c.T + t0, c.TTW),
                         start=False, stop=False)
                    E.mm(bankT(bku), hiu[:, jc * 128:(jc + 1) * 128],
                         Asl(A_xe + jc * c.T + t0, c.TTW),
                         start=(jc == 0), stop=False)
                    E.mm(bankT(bku), lou[:, jc * 128:(jc + 1) * 128],
                         Asl(A_xo + jc * c.T + t0, c.TTW),
                         start=False, stop=False)
                E.mm(bankT(bkg), szn_g[sl],
                     xsum_x[0:c.G, t0:t0 + c.TTW], start=False, stop=True)
                E.mm(bankT(bku), szn_u[sl],
                     xsum_x[0:c.G, t0:t0 + c.TTW], start=False, stop=True,
                     inc=[(sTM, 1)])
                mmdone[ic] = sTM.n
                gsb = bfstg[(2 * ic + tt) % 4][:]
                E.wait(s, sTM, mmdone[ic])
                prev = hdone.get((ic - 2, tt))
                if prev is not None:
                    E.wait(s, sVE, prev)
                E.op(s, "activation", gsb, bankT(bkg), AF.Silu, bias=0.0,
                     scale=1.0, inc=[(sAC, 1)])
                E.wait(v, sAC, sAC.n)
                if ic >= 2 and tt == 0:
                    E.wait(v, sHSs[ic % 2], hst_marks[ic - 2])
                if (ic - 2, tt) in xhmm_marks:
                    E.wait(v, sTM, xhmm_marks[(ic - 2, tt)])
                E.op(v, "tensor_tensor",
                     out=Asl(b_hstg[sl] + t0, c.TTW), in0=gsb, in1=bankT(bku),
                     op=ALU.mult, inc=[(sVE, 1)])
                hdone[(ic, tt)] = sVE.n
            E.wait(g, sVE, sVE.n)
            E.dma(g, h_d.ap()[ic * 128:(ic + 1) * 128, :],
                  Asl(b_hstg[sl], c.T), inc=[(sHSs[sl], 16)])
            hst_marks[ic] = sHSs[sl].n
        emit_xh(c.FIC - 1)
        marks["h_done_a"] = sHSs[0].n
        marks["h_done_b"] = sHSs[1].n
        E.wait(v, sVE, sVE.n)
        E.op(v, "tensor_copy", xh_bf0, xh0, inc=[(sVE, 1)])
        if c.DG_T1 > 0:
            E.op(v, "tensor_copy", xh_bf1, xh1, inc=[(sVE, 1)])
        marks["xh_bf"] = sVE.n
        marks["p6_done"] = sVE.n
        marks["p6_hs0"] = sHSs[0].n
        marks["p6_hs1"] = sHSs[1].n

    def p7_down_and_out():
        v, s, t, g = "vector", "scalar", "tensor", "gpsimd"
        WSL = c.FIC * 128
        assert 2 * WSL <= B_N
        ffn_done = {}
        for hf in range(c.NTT):
            t0 = hf * c.TTW
            row_reset()
            E.wait(g, sGD, sGD.n)
            E.wait(g, sHSs[0], marks["h_done_a"])
            E.wait(g, sHSs[1], marks["h_done_b"])
            E.wait(g, sVE, marks["p6_done"])
            E.wait(g, sTM, sTM.n)
            E.wait(g, sVE, marks.get("p7_free", 0))
            for ic in range(c.FIC):
                E.dma(g, Asl(ic * c.TTW, c.TTW),
                      h_d.ap()[ic * 128:(ic + 1) * 128, t0:t0 + c.TTW],
                      inc=[(sGD, 16)])
            he_mark = sGD.n
            evs = {}
            wld = {}
            ffnst_marks = {}
            p7st_marks = {}
            for hc in range(c.DC):
                sl = hc % 2
                if hc == 0 and hf == 0:
                    E.wait("sync", sVE, marks["p6_done"])
                    E.wait("sync", sTM, sTM.n)
                if hc >= 2:
                    E.wait("sync", sTM, evs[hc - 2])
                elif hf > 0:
                    E.wait("sync", sTM,
                           marks["p7mm_hm2"] if hc == 0 else marks["p7mm"])
                wsrc = AP(tensor=wdeq_d.ap().tensor,
                          offset=hc * c.FI * 128,
                          ap=[[128, 128], [128 * 128, c.FIC], [1, 128]])
                E.dma("sync",
                      Bsl(sl * WSL, WSL).rearrange(
                          "p (ic hh) -> p ic hh", hh=128),
                      wsrc, inc=[(sWLs[sl], 16)])
                wld[hc] = sWLs[sl].n
                E.wait("sync", sWLs[sl], sWLs[sl].n)
                if hc >= 1 or hf > 0:
                    E.wait("sync", sVE, marks.get("p7szn", 0))
                E.dma("sync", szsd[0],
                      sd.ap()[0:c.DG_T0, hc * 128:(hc + 1) * 128],
                      inc=[(sSY, 16)])
                E.dma("sync", szsd[1],
                      zd.ap()[0:c.DG_T0, hc * 128:(hc + 1) * 128],
                      inc=[(sSY, 16)])
                if c.DG_T1 > 0:
                    E.dma("sync", szsd[2][0:c.DG_T1, :],
                          sd.ap()[128:128 + c.DG_T1,
                                  hc * 128:(hc + 1) * 128],
                          inc=[(sSY, 16)])
                    E.dma("sync", szsd[3][0:c.DG_T1, :],
                          zd.ap()[128:128 + c.DG_T1,
                                  hc * 128:(hc + 1) * 128],
                          inc=[(sSY, 16)])
                szld = sSY.n
                E.wait(v, sSY, szld)
                if hc >= 2:
                    E.wait(v, sTM, evs[hc - 2])
                elif hf > 0:
                    E.wait(v, sTM,
                           marks["p7mm_hm2"] if hc == 0 else marks["p7mm"])
                E.op(v, "scalar_tensor_tensor", out=sznd0[sl],
                     in0=szsd[0], scalar=-1.0, in1=szsd[1],
                     op0=ALU.mult, op1=ALU.mult, inc=[(sVE, 1)])
                if c.DG_T1 > 0:
                    E.op(v, "scalar_tensor_tensor", out=sznd1[sl],
                         in0=szsd[2][0:c.DG_T1, :], scalar=-1.0,
                         in1=szsd[3][0:c.DG_T1, :], op0=ALU.mult,
                         op1=ALU.mult, inc=[(sVE, 1)])
                szn_mark = sVE.n
                marks["p7szn"] = szn_mark
                bk_i = hc % 2
                E.wait(t, sGD, he_mark)
                E.wait(t, sVE, szn_mark)
                E.wait(t, sVE, marks["xh_bf"])
                E.wait(t, sWLs[hc % 2], wld[hc])
                if hc >= 2:
                    E.wait(t, sVE, evs.get((hc - 2, "ev"), 0))
                elif hf > 0:
                    E.wait(t, sVE, marks.get("p7evv", 0))
                for ic in range(c.FIC):
                    E.mm(bankT(bk_i),
                         Bsl(sl * WSL + ic * 128, 128),
                         Asl(ic * c.TTW, c.TTW),
                         start=(ic == 0), stop=False)
                E.mm(bankT(bk_i), sznd0[sl],
                     xh_bf0[0:c.DG_T0, t0:t0 + c.TTW],
                     start=False, stop=(c.DG_T1 == 0),
                     inc=[(sTM, 1)] if c.DG_T1 == 0 else None)
                if c.DG_T1 > 0:
                    E.mm(bankT(bk_i), sznd1[sl][0:c.DG_T1, :],
                         xh_bf1[0:c.DG_T1, t0:t0 + c.TTW],
                         start=False, stop=True, inc=[(sTM, 1)])
                evs[hc] = sTM.n
                marks["p7mm"] = sTM.n
                if hc == c.DC - 2:
                    marks["p7mm_hm2"] = sTM.n
                E.wait(v, sTM, evs[hc])
                if hc >= 1:
                    E.wait(v, sGD, ffnst_marks[hc - 1])
                elif hf > 0:
                    E.wait(v, sGD, marks.get("p7ffnst", 0))
                fo = bfstg[2 + hc % 2][:]
                E.op(v, "tensor_copy", fo, bankT(bk_i), inc=[(sVE, 1)])
                evs[(hc, "ev")] = sVE.n
                E.wait(g, sVE, evs[(hc, "ev")])
                E.dma(g, ffn_d.ap()[hc * 128:(hc + 1) * 128, t0:t0 + c.TTW],
                      fo, inc=[(sGD, 16)])
                ffnst_marks[hc] = sGD.n
                ssq = sqstg[hc % 2][:]
                E.wait(v, sVE, evs[(hc, "ev")])
                if hc >= 2:
                    E.wait(v, sTM, p7st_marks[hc - 2])
                E.op(v, "tensor_tensor", out=ssq, in0=fo, in1=fo,
                     op=ALU.mult, inc=[(sVE, 1)])
                E.wait(t, sVE, sVE.n)
                E.mm(bank(6 if hf == 0 else 7, 0, 1, 0, c.TTW),
                     ones_bf[:], fo, start=(hc == 0), stop=(hc == c.DC - 1))
                E.mm(bank(4 if hf == 0 else 5, 0, 1, 0, c.TTW),
                     ones_bf[:], ssq, start=(hc == 0), stop=(hc == c.DC - 1),
                     inc=[(sTM, 1)])
                p7st_marks[hc] = sTM.n
            marks["p7evv"] = sVE.n
            marks["p7ffnst"] = sGD.n
            ffn_done[hf] = sGD.n
            marks["p7_free"] = sVE.n
        for tt in range(c.NTT):
            t0 = tt * c.TTW
            row_reset()
            E.wait(v, sTM, sTM.n)
            sum_r = row("p7_sum", c.TTW)
            sum2_r = row("p7_sum2", c.TTW)
            E.op(v, "tensor_copy", sum_r,
                 bank(6 if tt == 0 else 7, 0, 1, 0, c.TTW), inc=[(sVE, 1)])
            E.op(v, "tensor_copy", sum2_r,
                 bank(4 if tt == 0 else 5, 0, 1, 0, c.TTW), inc=[(sVE, 1)])
            E.wait(v, sVE, sVE.n)
            r_r, m2_r = row_rstd("p7", sum_r, sum2_r, c.D, c.EPS_LN)
            bc_mark = bounce_rows([(r_r, rbc[0][:]), (m2_r, mbc[0][:])], 0)
            E.wait(g, sGD, marks["p7ffnst"])
            out_marks = {}
            ost_marks = {}
            ost_base = sOUT.n
            for ic in range(c.DC):
                sl4 = ic % 4
                sl2 = ic % 2
                if ic >= 2:
                    E.wait(g, sVE, out_marks[ic - 2])
                elif tt > 0:
                    E.wait(g, sVE, marks.get("p7out_prev", 0))
                E.wait(g, sBC, bc_mark)
                E.dma(g, bfstg[sl4][:],
                      ffn_d.ap()[ic * 128:(ic + 1) * 128, t0:t0 + c.TTW],
                      inc=[(sBFs[sl4], 16)])
                E.dma(g, xstg[sl2][:],
                      xmid_d.ap()[ic * 128:(ic + 1) * 128, t0:t0 + c.TTW],
                      inc=[(sXSs[sl2], 16)])
                E.wait(v, sBFs[sl4], sBFs[sl4].n)
                E.wait(v, sXSs[sl2], sXSs[sl2].n)
                E.op(v, "tensor_tensor", out=f32stg[sl2][:],
                     in0=bfstg[sl4][:], in1=rbc[0][:], op=ALU.mult,
                     inc=[(sVE, 1)])
                E.wait(v, sVE, sVE.n)
                E.op(v, "tensor_tensor", out=f32stg[sl2][:],
                     in0=f32stg[sl2][:], in1=mbc[0][:], op=ALU.add,
                     inc=[(sVE, 1)])
                E.wait(v, sVE, sVE.n)
                E.wait(v, sOUT, ost_marks.get(ic - 1, ost_base))
                E.op(v, "tensor_tensor", out=f32stg[2 + sl2][:],
                     in0=f32stg[sl2][:], in1=xstg[sl2][:], op=ALU.add,
                     inc=[(sVE, 1)])
                out_marks[ic] = sVE.n
                E.wait(g, sVE, out_marks[ic])
                E.dma(g, outT.ap()[ic * 128:(ic + 1) * 128, t0:t0 + c.TTW],
                      f32stg[2 + sl2][:], inc=[(sOUT, 16)])
                ost_marks[ic] = sOUT.n
            marks["p7out_prev"] = sVE.n

    def epilogue():
        E.wait("gpsimd", sOUT, sOUT.n)

    def program():
        marks.clear()
        row_reset()
        wblk[0] = 0
        wd_state["i"] = 0
        wd_state["nib"] = {}
        wd_state["ld"] = {}
        wd_state["st"] = {}
        prologue()
        p1_ln1()
        p2_qkv()
        p3_collect()
        p4_attention()
        p5_wo_ln()
        p5b_scales_and_gathers()
        p6_gate_up()
        p7_down_and_out()
        epilogue()

    with nc.Block() as block:
        for eng in Emitter.ENGS:
            def mk(e):
                def f(_engine):
                    E.role = e
                    E.reset()
                    program()
                return f
            getattr(block, eng)(mk(eng))

    ctx.close()
    return nc, c


def prep_inputs(c, kw):
    f = np.asarray
    x = f(kw["L0_attn_hidden_states"], dtype=np.float32).reshape(c.S, c.D)
    xTf = np.ascontiguousarray(x.T)

    def blk_imaj(W):
        W = f(W, dtype=np.float32)
        return np.ascontiguousarray(
            W.reshape(c.DC, 128, c.DC, 128).transpose(2, 1, 0, 3))

    def colvec(b):
        return np.ascontiguousarray(
            f(b, dtype=np.float32).reshape(c.DC, 128).T)

    def pack_gu(Wq):
        Wt = f(Wq, dtype=np.int32).T
        return np.ascontiguousarray(
            Wt.reshape(c.FJC, 128, c.FIC, 128).transpose(2, 1, 0, 3))

    def pack_d(Wq):
        Wt = f(Wq, dtype=np.int32).T
        return np.ascontiguousarray(
            Wt.reshape(c.DJC, 128, c.DC, 128).transpose(2, 1, 0, 3))

    def tr(a):
        return np.ascontiguousarray(f(a, dtype=np.float32).T)

    shared = {
        "wq": blk_imaj(kw["L0_attn_Wq"]),
        "wk": blk_imaj(kw["L0_attn_Wk"]),
        "wv": blk_imaj(kw["L0_attn_Wv"]),
        "wo": blk_imaj(kw["L0_attn_Wo"]),
        "bq": colvec(kw["L0_attn_bq"]),
        "bk": colvec(kw["L0_attn_bk"]),
        "bv": colvec(kw["L0_attn_bv"]),
        "bo": colvec(kw["L0_attn_bo"]),
        "gq": colvec(kw["L0_attn_gq"]),
        "gk": colvec(kw["L0_attn_gk"]),
        "wgp": pack_gu(kw["L0_ffn_gate_Wq"]),
        "wup": pack_gu(kw["L0_ffn_up_Wq"]),
        "wdp": pack_d(kw["L0_ffn_down_Wq"]),
        "sg": tr(kw["L0_ffn_gate_scale"]),
        "zg": tr(kw["L0_ffn_gate_zero"]),
        "su": tr(kw["L0_ffn_up_scale"]),
        "zu": tr(kw["L0_ffn_up_zero"]),
        "sd": tr(kw["L0_ffn_down_scale"]),
        "zd": tr(kw["L0_ffn_down_zero"]),
    }
    in_maps = []
    for core in range(NCORES):
        m = dict(shared)
        m["xT"] = np.ascontiguousarray(xTf[:, core * c.T:(core + 1) * c.T])
        in_maps.append(m)
    return in_maps


def kernel(**kw):
    c = Cfg(full_cfg())
    nc, _ = build_program(full_cfg())
    in_maps = prep_inputs(c, kw)
    res = run_bass_kernel_spmd(nc, in_maps, core_ids=list(range(NCORES)))
    outs = [np.asarray(r["outT"]) for r in res.results]
    full = np.concatenate(outs, axis=1)
    return np.ascontiguousarray(full.T).reshape(1, c.S, c.D).astype(np.float32)
